# version 06
# baseline (speedup 1.0000x reference)
import numpy as np
import ml_dtypes
import concourse.bass as bass
import concourse.mybir as mybir
from concourse.bass_utils import run_bass_kernel_spmd

F32 = mybir.dt.float32
BF16 = mybir.dt.bfloat16
AF = mybir.ActivationFunctionType
ALU = mybir.AluOpType
AX = mybir.AxisListType
BFD = ml_dtypes.bfloat16


class Sem:
    def __init__(self, name):
        self.name = name
        self.count = 0
        self.h = None

class Tok:
    __slots__ = ("sem", "val")
    def __init__(self, sem, val):
        self.sem = sem
        self.val = val

class Plan:
    def __init__(self):
        self.ops = []
        self.sems = {}

    def sem(self, name):
        if name not in self.sems:
            self.sems[name] = Sem(name)
        return self.sems[name]

    def op(self, engine, emit_fn, waits=(), inc=None, inc_n=1):
        waits = [w for w in waits if w is not None]
        tok = None
        incs = []
        if inc is not None:
            inc.count += inc_n
            tok = Tok(inc, inc.count)
            incs.append((inc, inc_n))
        self.ops.append((engine, emit_fn, waits, incs))
        return tok

    def boundary(self):
        return [Tok(s, s.count) for s in self.sems.values()]

    def emit(self, nc, es):
        for s in self.sems.values():
            s.h = es.enter_context(nc.semaphore(s.name))
        blk = es.enter_context(nc.Block())
        by_eng = {}
        for rec in self.ops:
            by_eng.setdefault(rec[0], []).append(rec)

        def run_engine(eng_obj, recs):
            watermark = {}
            for _, emit_fn, waits, incs in recs:
                by_sem = {}
                for w in waits:
                    if by_sem.get(w.sem.name, (None, -1))[1] < w.val:
                        by_sem[w.sem.name] = (w.sem, w.val)
                for name, (sem, val) in by_sem.items():
                    if watermark.get(name, -1) >= val:
                        continue
                    eng_obj.wait_ge(sem.h, val)
                    watermark[name] = val
                instr = emit_fn(eng_obj)
                for s, n in incs:
                    instr.then_inc(s.h, n)

        for name in ("sync", "scalar", "vector", "tensor", "gpsimd"):
            if name not in by_eng:
                continue
            recs = by_eng[name]
            section = getattr(blk, name)
            def make(rs):
                def _f(eng_obj):
                    run_engine(eng_obj, rs)
                return _f
            section(make(recs))
        return blk

class Buf:
    __slots__ = ("writers", "readers")
    def __init__(self):
        self.writers = []
        self.readers = []

    def rd_waits(self):
        return list(self.writers)

    def add_reader(self, tok):
        if tok is not None:
            self.readers.append(tok)

    def wr_waits(self):
        return self.readers + self.writers

    def set_writer(self, tok):
        self.writers = [tok] if tok is not None else []
        self.readers = []

    def add_writer(self, tok):
        if tok is not None:
            self.writers.append(tok)


def make_cfg(mini=False):
    c = {}
    if mini == "midi":
        c.update(D=2048, E=16, IB=2, SHQ=2, M=2048, PFX=512, ROWW=512)
    elif mini:
        c.update(D=512, E=4, IB=2, SHQ=2, M=1024, PFX=512, ROWW=128)
    else:
        c.update(D=2048, E=16, IB=11, SHQ=4, M=2048, PFX=2048, ROWW=512)
    D = c["D"]
    c["FC"] = D // 128
    c["H"] = D // 256
    c["SC"] = 512
    c["CPS"] = 4
    c["NSC"] = c["M"] // c["SC"]
    c["NSCP"] = c["PFX"] // c["SC"]
    c["G"] = c["M"] // 2
    c["SG"] = c["G"] // 128
    c["FOD"] = D // 512
    c["GHALF"] = [(h * 512, min(512, c["G"] - h * 512)) for h in range((c["G"] + 511) // 512)]
    c["NCH"] = c["M"] // 128
    c["C"] = min(384, c["G"])
    if c["E"] <= 4:
        c["C"] = c["G"]
    return c


def build_program(cfg, debug_spill=False):
    D, FC, H, E, IB, SHQ = cfg["D"], cfg["FC"], cfg["H"], cfg["E"], cfg["IB"], cfg["SHQ"]
    M, PFX, SC, G, SG, FOD = cfg["M"], cfg["PFX"], cfg["SC"], cfg["G"], cfg["SG"], cfg["FOD"]
    NSC, NSCP, ROWW, CPS = cfg["NSC"], cfg["NSCP"], cfg["ROWW"], cfg["CPS"]
    GHALF, NCH = cfg["GHALF"], cfg["NCH"]
    NHALF = len(GHALF)
    C = cfg["C"]
    CC = 2 * C
    CHALF = [(h * 512, min(512, CC - h * 512)) for h in range((CC + 511) // 512)]
    NCHALF = len(CHALF)
    CSUB = CC // 128
    EW = E + 1
    HP = H * 128
    NBO = (2 * HP) // 512 if 2 * HP >= 512 else 1
    NAB = (H + 3) // 4
    NP = (HP + ROWW - 1) // ROWW
    SCALING = 0.0625
    S16 = 1.0 / 16.0

    nc = bass.Bass("TRN2", target_bir_lowering=False)

    def din(name, shape, dt=F32):
        return nc.dram_tensor(name, shape, dt, kind="ExternalInput").ap()

    d_xfm = din("xfm", [NSC, 128, FC * SC])
    d_xpfm = din("xpfm", [max(NSCP, 1), 128, FC * SC])
    d_xtm = din("xtm", [NCH, 128, D])
    d_wq = din("wq", [H, 128, FC * 128])
    d_wk = din("wk", [H, 128, FC * 128])
    d_wkg1 = din("wkg1", [128, FC * 16])
    d_wkg2 = din("wkg2", [16, H * 128])
    d_bkg2 = din("bkg2s", [128, H])
    d_wv = din("wv", [FOD, FC // 4, 128, 4 * 512])
    d_wg = din("wg", [FC, 128, FC * 128])
    d_bg = din("bgs", [128, FC])
    d_wo = din("wo", [FOD, FC // 4, 128, 4 * 512])
    d_rwt = din("rwt", [128, FC * EW])
    d_wgate = din("wgate", [E, IB, 128, FC * 128])
    d_wup = din("wup", [E, IB, 128, FC * 128])
    d_wdown = din("wdown", [E, FOD, 128, IB * 512])
    d_shg = din("shg", [SHQ * IB, 128, FC * 128])
    d_shu = din("shu", [SHQ * IB, 128, FC * 128])
    d_shd = din("shd", [SHQ, FOD, 128, IB * 512])
    d_iota16 = din("iota16", [128, 16])
    d_identf = din("identf", [128, 128])
    d_masku = din("masku4", [128, 4 * 128])
    d_ident = din("identbf", [128, 128], BF16)

    d_out = nc.dram_tensor("out", [M, D], F32, kind="ExternalOutput").ap()
    spill_kind = "ExternalOutput" if debug_spill else "Internal"
    d_x2tm = nc.dram_tensor("x2tm_sp", [M, D], F32, kind=spill_kind).ap()
    d_x2bf = nc.dram_tensor("x2bf_sp", [M, D], BF16, kind=spill_kind).ap()
    d_xc = nc.dram_tensor("xc_sp", [E * CC, D], BF16, kind=spill_kind).ap()
    d_yc = nc.dram_tensor("yc_sp", [E * CC, D], BF16, kind=spill_kind).ap()

    from contextlib import ExitStack
    es = ExitStack()
    with es:
        def mklay(pairs):
            lay, off = {}, 0
            for name, w in pairs:
                w = (w + 15) // 16 * 16
                lay[name] = (off, w)
                off += w
            return lay, off

        bf_attn, n1 = mklay([
            ("xsb", FC * SC), ("qd", H * SC), ("kneg", H * SC), ("kdfm", H * SC),
            ("kdtm", H * 128), ("vtm0", D), ("vtm1", D),
            ("silug", FC * 256), ("og0", FC * 128), ("og1", FC * 128),
            ("Asb", H * 128), ("Sbf", H * 256), ("kg1", 512),
            ("wp0", max(FC * 128, 2048)), ("wp1", max(FC * 128, 2048)), ("wkg2sb", H * 128),
            ("obf", H * 256), ("sqbf", H * 256), ("rowbf", 2 * HP),
            ("x2bfc", D), ("identb", 128), ("wkg1sb", FC * 16),
        ])
        bf_ffn, n2 = mklay([
            ("x2sb", max(FC * G, FC * CC)), ("tsb", max(IB * G, IB * CC)),
            ("xtmrow", D), ("keepbf", 16), ("ygat", 2 * D), ("identb", 128),
            ("wgb0", FC * 128), ("wgb1", FC * 128),
            ("wub0", FC * 128), ("wub1", FC * 128),
            ("wdb0", IB * 512), ("wdb1", IB * 512),
            ("rwtsb", FC * EW),
        ])
        NBF = max(n1, n2) + 16

        f32_attn, m1 = mklay([
            ("zc", H * 128), ("scr1", H * 128), ("scr2", H * 128),
            ("gkc", max(H * SC, 3 * HP)),
            ("ehs", 2 * SC), ("dl", 64), ("gkcl", 64),
            ("xtmc0", D), ("xtmc1", D), ("rc0", D), ("rc1", D),
            ("x2c", D), ("rows", 2 * HP), ("lnv", 64),
            ("S", H * 256), ("masku", 4 * 128), ("onesf", 128),
            ("bkg2", 16), ("bgc", FC), ("eps", 16),
        ])
        f32_ffn, m2 = mklay([
            ("routed", SG * D), ("siluscr", NHALF * 512), ("smx", 2 * (6 * 16 + 64)),
            ("comb", 2 * SG * EW), ("r2", D), ("x2tmc", D), ("out2", D), ("lnv2", 64),
            ("eps2", 16), ("eqsv", SG * 4 * 16), ("wsave", 2 * SG * 4),
            ("rktm", 32), ("iota16", 16), ("sprs", 32), ("identf", 128), ("onesf2", 128),
        ])
        NF32 = max(m1, m2) + 16

        abf = es.enter_context(nc.sbuf_tensor("abf", [128, NBF], BF16))
        af32 = es.enter_context(nc.sbuf_tensor("af32", [128, NF32], F32))
        onesbc = es.enter_context(nc.sbuf_tensor("onesbc", [128, 1], BF16))
        U32 = mybir.dt.uint32
        idxsb = es.enter_context(nc.sbuf_tensor("idxsb", [128, 2 * SG * 4], U32))
        onesbr = es.enter_context(nc.sbuf_tensor("onesbr", [1, 128], BF16))

        PHASE = {"bf": bf_attn, "f32": f32_attn}

        def B(name, lo=0, w=None, p0=0, pn=128):
            o, ww = PHASE["bf"][name]
            if w is None:
                w = ww - lo
            assert lo + w <= ww, (name, lo, w, ww)
            return abf[p0:p0 + pn, o + lo: o + lo + w]

        def F(name, lo=0, w=None, p0=0, pn=128):
            o, ww = PHASE["f32"][name]
            if w is None:
                w = ww - lo
            assert lo + w <= ww, (name, lo, w, ww)
            return af32[p0:p0 + pn, o + lo: o + lo + w]

        def Fap(name, lo, dims):
            o, _ = PHASE["f32"][name]
            return bass.AP(af32, o + lo, [[NF32, 128]] + dims)

        def Bap(name, lo, dims):
            o, _ = PHASE["bf"][name]
            return bass.AP(abf, o + lo, [[NBF, 128]] + dims)

        pbbf = es.enter_context(nc.psum_tensor("pbbf", [128, 1024], BF16))
        pball = es.enter_context(nc.psum_tensor("pball", [128, 3584], F32))

        def bank(i, lo=0, w=512, p0=0, pn=128):
            assert 1 <= i <= 7 and lo + w <= 512
            return pball[p0:p0 + pn, (i - 1) * 512 + lo:(i - 1) * 512 + lo + w]

        P = Plan()
        pes = P.sem("pe")
        dve = P.sem("dve")
        act = P.sem("act")
        sts = P.sem("st")
        lds = P.sem("lds")
        gsem = {}
        def gs(name):
            if name not in gsem:
                gsem[name] = P.sem("g_" + name)
            return gsem[name]

        bank_buf = {i: Buf() for i in range(8)}
        bufs = {}
        def SB(name):
            if name not in bufs:
                bufs[name] = Buf()
            return bufs[name]

        def pop(engine, emit_fn, waits=(), inc=None, inc_n=1):
            b_, f_ = PHASE["bf"], PHASE["f32"]
            def wrapped(e, emit_fn=emit_fn, b_=b_, f_=f_):
                PHASE["bf"], PHASE["f32"] = b_, f_
                return emit_fn(e)
            return P.op(engine, wrapped, waits=waits, inc=inc, inc_n=inc_n)

        def gload(stream, dst_ap, src_ap, buf, multi=False):
            tok = pop("gpsimd", (lambda d, s: lambda g: g.dma_start(out=d, in_=s))(dst_ap, src_ap),
                       waits=buf.wr_waits(), inc=gs(stream), inc_n=16)
            if multi:
                buf.add_writer(tok)
            else:
                buf.set_writer(tok)
            return tok

        def sload(dst_ap, src_ap, buf, transpose=False, extra=(), multi=False, stream="s"):
            if transpose:
                fn = (lambda d, s: lambda e: e.dma_start_transpose(out=d, in_=s))(dst_ap, src_ap)
            else:
                fn = (lambda d, s: lambda e: e.dma_start(out=d, in_=s))(dst_ap, src_ap)
            tok = pop("sync", fn, waits=list(extra) + (buf.writers if multi else buf.wr_waits()),
                       inc=gs("sl_" + stream), inc_n=16)
            if multi:
                buf.add_writer(tok)
            else:
                buf.set_writer(tok)
            return tok

        def store(dst_ap, src_buf, src_ap, stream="o"):
            tok = pop("sync", (lambda d, s: lambda e: e.dma_start(out=d, in_=s))(dst_ap, src_ap),
                       waits=src_buf.rd_waits(), inc=gs("st_" + stream), inc_n=16)
            src_buf.add_reader(tok)
            return tok

        open_grp = {}

        def mm(bk_buf, out_ap, in_bufs, lhsT, rhs, first, last, transpose_in=None,
               force_tok=False):
            waits = []
            if first:
                waits += bk_buf.wr_waits()
            for b in in_bufs:
                waits += b.rd_waits()
            if transpose_in is not None:
                fn = (lambda o, l, i: lambda t: t.transpose(o, l, i))(out_ap, lhsT, transpose_in)
            else:
                fn = (lambda o, l, r, f, s: lambda t: t.matmul(o, l, r, start=f, stop=s))(
                    out_ap, lhsT, rhs, first, last)
            tok = pop("tensor", fn, waits=waits, inc=(pes if (last or force_tok) else None))
            key = id(bk_buf)
            if first:
                open_grp[key] = []
            if tok is not None:
                for b in in_bufs:
                    b.add_reader(tok)
            else:
                open_grp.setdefault(key, []).extend(in_bufs)
            if last:
                grp = open_grp.pop(key, [])
                seen = set()
                for b in grp:
                    if id(b) not in seen:
                        seen.add(id(b))
                        b.add_reader(tok)
                bk_buf.set_writer(tok)
            return tok

        def vop(fn, reads, writes, waits=(), engine="vector", multi_write=False):
            sem = act if engine == "scalar" else dve
            ws = list(waits)
            for b in reads:
                ws += b.rd_waits()
            for b in writes:
                ws += (b.writers if multi_write else b.wr_waits())
            tok = pop(engine, fn, waits=ws, inc=sem)
            for b in reads:
                b.add_reader(tok)
            for b in writes:
                if multi_write:
                    b.add_writer(tok)
                else:
                    b.set_writer(tok)
            return tok

        vop(lambda v: v.memset(onesbc[:], 1.0), [], [SB("onesbc")])
        vop(lambda v: v.memset(onesbr[:], 1.0), [], [SB("onesbr")])
        vop(lambda v: v.memset(F("onesf"), 1.0), [], [SB("onesf")])
        vop(lambda v: v.memset(F("S"), 0.0), [], [SB("S")])
        sload(F("masku"), d_masku, SB("masku"), stream="masku")
        sload(F("bkg2", 0, H), d_bkg2, SB("bkg2"), stream="bkg2")
        sload(F("bgc", 0, FC), d_bg, SB("bgc"), stream="bgc")
        vop(lambda v: v.memset(F("eps", 0, 1), 1e-5), [], [SB("eps")])
        gload("ident", B("identb"), d_ident, SB("identb"))
        gload("wkg1", B("wkg1sb", 0, FC * 16), d_wkg1, SB("wkg1"))
        gload("wkg2", B("wkg2sb", p0=0, pn=16), d_wkg2, SB("wkg2"))

        wp_slot = ["wp0", "wp1"]

        NQ = FC // 4

        def emit_attn_superchunk(sci, main):
            x_src = d_xfm[sci] if main else d_xpfm[sci]
            for xq in range(NQ):
                gload("xsb%d" % xq, B("xsb", xq * 4 * SC, 4 * SC),
                      x_src[:, xq * 4 * SC:(xq + 1) * 4 * SC], SB("xsb%d" % xq))

            for fc in range(FC):
                mm(bank_buf[7], bank(7, p0=0, pn=16, w=SC),
                   [SB("wkg1"), SB("xsb%d" % (fc // 4))],
                   B("wkg1sb", fc * 16, 16),
                   B("xsb", fc * SC, SC),
                   fc == 0, fc == FC - 1)
            vop(lambda v: v.tensor_copy(B("kg1", 0, SC, p0=0, pn=16), bank(7, p0=0, pn=16, w=SC)),
                [bank_buf[7]], [SB("kg1")])

            for c in range(CPS):
                bks = [1, 2] if c % 2 == 0 else [3, 4]
                for h in range(H):
                    bk = bks[h // 4]
                    q4 = h % 4
                    nh = min(4, H - (h // 4) * 4)
                    mm(bank_buf[bk], bank(bk, q4 * 128, 128),
                       [SB("wkg2"), SB("kg1")],
                       B("wkg2sb", h * 128, 128, p0=0, pn=16),
                       B("kg1", c * 128, 128, p0=0, pn=16),
                       q4 == 0, q4 == nh - 1)
                for h in range(H):
                    bk = bks[h // 4]
                    vop((lambda h, bk: lambda v: v.tensor_scalar(
                        F("zc", h * 128, 128), bank(bk, (h % 4) * 128, 128),
                        F("bkg2", h, 1), None, op0=ALU.add))(h, bk),
                        [bank_buf[bk], SB("bkg2")], [SB("zc")])
                vop(lambda sc: sc.activation(F("scr1", 0, HP), F("zc", 0, HP), AF.Abs),
                    [SB("zc")], [SB("scr1")], engine="scalar")
                vop(lambda sc: sc.activation(F("scr2", 0, HP), F("scr1", 0, HP), AF.Exp, scale=-1.0),
                    [SB("scr1")], [SB("scr2")], engine="scalar")
                vop(lambda sc: sc.activation(F("scr1", 0, HP), F("scr2", 0, HP), AF.Ln, bias=1.0),
                    [SB("scr2")], [SB("scr1")], engine="scalar")
                vop(lambda v: v.scalar_tensor_tensor(
                    F("scr2", 0, HP), F("zc", 0, HP), 0.0, F("scr1", 0, HP),
                    op0=ALU.min, op1=ALU.subtract),
                    [SB("zc"), SB("scr1")], [SB("scr2")])
                for h in range(H):
                    vop((lambda h, c: lambda v: v.tensor_tensor_scan(
                        F("gkc", h * SC + c * 128, 128), F("onesf"),
                        F("scr2", h * 128, 128), 0.0, op0=ALU.mult, op1=ALU.add))(h, c),
                        [SB("onesf"), SB("scr2")], [SB("gkc")])

            last_ap = Fap("gkc", 127, [[SC, H], [128, CPS]])
            vop(lambda sc: sc.activation(F("dl", 0, H * CPS), last_ap, AF.Exp, scale=S16),
                [SB("gkc")], [SB("dl")], engine="scalar")
            vop(lambda v: v.tensor_scalar(F("gkcl", 0, H * CPS), last_ap, S16, None, op0=ALU.mult),
                [SB("gkc")], [SB("gkcl")])

            if main:
                for oc in range(H):
                    p = oc % 2
                    slot = wp_slot[p]
                    gload("wp%d" % p, B(slot, 0, FC * 128), d_wq[oc], SB(slot))
                    bk = [5, 6, 7][oc % 3]
                    for fc in range(FC):
                        mm(bank_buf[bk], bank(bk),
                           [SB(slot), SB("xsb%d" % (fc // 4))],
                           B(slot, fc * 128, 128),
                           B("xsb", fc * SC, SC),
                           fc == 0, fc == FC - 1)
                    ep = oc % 2
                    vop((lambda oc, ep: lambda sc: sc.activation(
                        F("ehs", ep * SC, SC), F("gkc", oc * SC, SC), AF.Exp, scale=S16))(oc, ep),
                        [SB("gkc")], [SB("ehs%d" % ep)], engine="scalar")
                    vop((lambda oc, bk, ep: lambda v: v.tensor_tensor(
                        B("qd", oc * SC, SC), F("ehs", ep * SC, SC), bank(bk), op=ALU.mult))(oc, bk, ep),
                        [SB("ehs%d" % ep), bank_buf[bk]], [SB("qd")])

            for oc in range(H):
                p = oc % 2
                slot = wp_slot[p]
                gload("wp%d" % p, B(slot, 0, FC * 128), d_wk[oc], SB(slot))
                bk = [5, 6, 7][oc % 3] if not main else [1, 2, 3][oc % 3]
                for fc in range(FC):
                    mm(bank_buf[bk], bank(bk),
                       [SB(slot), SB("xsb%d" % (fc // 4))],
                       B(slot, fc * 128, 128),
                       B("xsb", fc * SC, SC),
                       fc == 0, fc == FC - 1)
                ep = oc % 2
                if main:
                    vop((lambda oc, ep: lambda sc: sc.activation(
                        F("ehs", ep * SC, SC), F("gkc", oc * SC, SC), AF.Exp, scale=-S16))(oc, ep),
                        [SB("gkc")], [SB("ehs%d" % ep)], engine="scalar")
                    vop((lambda oc, bk, ep: lambda v: v.scalar_tensor_tensor(
                        B("kneg", oc * SC, SC), bank(bk), SCALING, F("ehs", ep * SC, SC),
                        op0=ALU.mult, op1=ALU.mult))(oc, bk, ep),
                        [SB("ehs%d" % ep), bank_buf[bk]], [SB("kneg")])
                for c in range(CPS):
                    vop((lambda oc, c, ep: lambda sc: sc.activation(
                        F("ehs", ep * SC + c * 128, 128), F("gkc", oc * SC + c * 128, 128), AF.Exp,
                        scale=-S16, bias=F("gkcl", oc * CPS + c, 1)))(oc, c, ep),
                        [SB("gkc"), SB("gkcl")], [SB("ehs%d" % ep)], engine="scalar")
                vop((lambda oc, bk, ep: lambda v: v.scalar_tensor_tensor(
                    B("kdfm", oc * SC, SC), bank(bk), SCALING, F("ehs", ep * SC, SC),
                    op0=ALU.mult, op1=ALU.mult))(oc, bk, ep),
                    [SB("ehs%d" % ep), bank_buf[bk]], [SB("kdfm")])

            def emit_g(hc):
                c0 = 2 * hc
                if main:
                    for oc in range(FC):
                        p = oc % 2
                        slot = wp_slot[p]
                        gload("wp%d" % p, B(slot, 0, FC * 128), d_wg[oc], SB(slot))
                        bk = [5, 6, 7][oc % 3]
                        for fc in range(FC):
                            mm(bank_buf[bk], bank(bk, 0, 256),
                               [SB(slot), SB("xsb%d" % (fc // 4))],
                               B(slot, fc * 128, 128),
                               B("xsb", fc * SC + c0 * 128, 256),
                               fc == 0, fc == FC - 1)
                        vop((lambda oc, bk: lambda sc: sc.activation(
                            B("silug", oc * 256, 256), bank(bk, 0, 256), AF.Silu,
                            bias=F("bgc", oc, 1)))(oc, bk),
                            [bank_buf[bk], SB("bgc")], [SB("silug")], engine="scalar")
            def emit_v(hc):
                c0 = 2 * hc
                for vb in range(FOD):
                    for q in range(FC // 4):
                        p = q % 2
                        slot = wp_slot[p]
                        gload("wp%d" % p, B(slot), d_wv[vb, q], SB(slot))
                        for sl in range(2):
                            c = c0 + sl
                            bk = 1 + ((vb % 2) * 2 + sl)
                            for f4 in range(4):
                                fc = 4 * q + f4
                                mm(bank_buf[bk], bank(bk),
                                   [SB(slot), SB("xsb%d" % (fc // 4))],
                                   B("xsb", fc * SC + c * 128, 128),
                                   B(slot, f4 * 512, 512),
                                   q == 0 and f4 == 0, q == FC // 4 - 1 and f4 == 3,
                                   force_tok=(sl == 1 and f4 == 3))
                    for sl in range(2):
                        bk = 1 + ((vb % 2) * 2 + sl)
                        vop((lambda sl, vb, bk: lambda v: v.tensor_copy(
                            B("vtm%d" % sl, vb * 512, 512), bank(bk)))(sl, vb, bk),
                            [bank_buf[bk]], [SB("vtm%d" % sl)])


            emit_g(0)
            emit_v(0)
            for sl in range(2):
                emit_gla_chunk(sci, 0, sl, main)
            emit_v(1)
            if main:
                emit_oproj_ln(sci, 0)
            emit_g(1)
            for sl in range(2):
                emit_gla_chunk(sci, 1, sl, main)
            if main:
                emit_oproj_ln(sci, 1)

        def emit_gla_chunk(sci, hc, sl, main):
            c = 2 * hc + sl
            vtm = "vtm%d" % sl
            for h in range(H):
                mm(bank_buf[0], pbbf[:, h * 128:(h + 1) * 128],
                   [SB("kdfm"), SB("identb")],
                   B("kdfm", h * SC + c * 128, 128),
                   None, h == 0, h == H - 1, transpose_in=B("identb"))
            vop(lambda v: v.tensor_copy(B("kdtm", 0, H * 128), pbbf[:, 0:H * 128]),
                [bank_buf[0]], [SB("kdtm")])

            if main:
                vop(lambda v: v.tensor_copy(B("Sbf"), F("S")),
                    [SB("S")], [SB("Sbf")])
                for h in range(H):
                    bk = 5 + h // 4
                    q4 = h % 4
                    nh = min(4, H - (h // 4) * 4)
                    mm(bank_buf[bk], bank(bk, q4 * 128, 128),
                       [SB("kneg"), SB("qd")],
                       B("kneg", h * SC + c * 128, 128),
                       B("qd", h * SC + c * 128, 128),
                       q4 == 0, q4 == nh - 1)
                for ab in range(NAB):
                    nh = min(4, H - ab * 4)
                    vop((lambda ab, nh: lambda v: v.tensor_tensor(
                        B("Asb", ab * 512, nh * 128), bank(5 + ab, 0, nh * 128),
                        F("masku", 0, nh * 128), op=ALU.mult))(ab, nh),
                        [bank_buf[5 + ab], SB("masku")], [SB("Asb")])
                cnt = {}
                total = {}
                for h in range(H):
                    for vdc in range(2):
                        bk = 1 + (vdc * HP + h * 128) // 512
                        total[bk] = total.get(bk, 0) + 2
                for h in range(H):
                    for vdc in range(2):
                        bk = 1 + (vdc * HP + h * 128) // 512
                        lo = (vdc * HP + h * 128) % 512
                        n = cnt.get(bk, 0)
                        mm(bank_buf[bk], bank(bk, lo, 128),
                           [SB(vtm), SB("Asb")],
                           B(vtm, h * 256 + vdc * 128, 128),
                           B("Asb", h * 128, 128),
                           n == 0, n + 1 == total[bk])
                        cnt[bk] = n + 1
                        n = cnt[bk]
                        mm(bank_buf[bk], bank(bk, lo, 128),
                           [SB("Sbf"), SB("qd")],
                           B("Sbf", h * 256 + vdc * 128, 128),
                           B("qd", h * SC + c * 128, 128),
                           n == 0, n + 1 == total[bk])
                        cnt[bk] = n + 1
            for h in range(H):
                bk = 7 if h % 2 == 0 else 6
                mm(bank_buf[bk], bank(bk, 0, 256),
                   [SB("kdtm"), SB(vtm)],
                   B("kdtm", h * 128, 128),
                   B(vtm, h * 256, 256),
                   True, True)
                vop((lambda h, c, bk: lambda v: v.scalar_tensor_tensor(
                    F("S", h * 256, 256), F("S", h * 256, 256),
                    F("dl", h * CPS + c, 1), bank(bk, 0, 256),
                    op0=ALU.mult, op1=ALU.add))(h, c, bk),
                    [SB("dl"), bank_buf[bk]], [SB("S")])

            if not main:
                return

            vop(lambda v: v.tensor_copy(B("obf"), pball[:, 0:2 * HP]),
                [bank_buf[b] for b in range(1, 1 + NBO)], [SB("obf")])
            vop(lambda sc: sc.activation(B("sqbf"), B("obf"), AF.Square),
                [SB("obf")], [SB("sqbf")], engine="scalar")
            NBK2 = (HP + 511) // 512
            rowmm = []
            kindless_off = {}
            for kind in range(2):
                src_n = "obf" if kind == 0 else "sqbf"
                for part in range(NP):
                    w = min(ROWW, HP - part * ROWW)
                    bk = 1 + kind * NBK2 + (part * ROWW) // 512
                    lo = (part * ROWW) % 512
                    kindless_off[(bk, lo)] = part * ROWW
                    for vdc in range(2):
                        rowmm.append((bk, lo, w, src_n, vdc))
            per_bank = {}
            for bk, *_ in rowmm:
                per_bank[bk] = per_bank.get(bk, 0) + 1
            seen = {}
            for bk, lo, w, src_n, vdc in rowmm:
                n = seen.get(bk, 0)
                part_off = kindless_off[(bk, lo)]
                mm(bank_buf[bk], bank(bk, lo, w, p0=0, pn=1),
                   [SB("onesbc"), SB(src_n)],
                   onesbc[:],
                   B(src_n, vdc * HP + part_off, w),
                   n == 0, n + 1 == per_bank[bk])
                seen[bk] = n + 1
            vop(lambda v: v.tensor_copy(B("rowbf", 0, 2 * HP, p0=0, pn=1),
                                        pball[0:1, 0:2 * HP]),
                [bank_buf[b] for b in range(1, 1 + 2 * NBK2)], [SB("rowbf")])
            for kind in range(2):
                for part in range(NP):
                    w = min(ROWW, HP - part * ROWW)
                    bk = 1 + kind * NBK2 + (part * ROWW) // 512
                    lo = (part * ROWW) % 512
                    first = (part * ROWW) % 512 == 0
                    last = (part == NP - 1) or ((part + 1) * ROWW) % 512 == 0
                    mm(bank_buf[bk], bank(bk, lo, w),
                       [SB("onesbr"), SB("rowbf")],
                       onesbr[:],
                       B("rowbf", kind * HP + part * ROWW, w, p0=0, pn=1),
                       first, last)
            MEAN, RSTD, RSG = 0, HP, 2 * HP
            sumbanks = [bank_buf[b] for b in range(1, 1 + NBK2)]
            sqbanks = [bank_buf[b] for b in range(1 + NBK2, 1 + 2 * NBK2)]
            vop(lambda v: v.tensor_scalar(F("gkc", MEAN, HP), pball[:, 0:HP],
                                          1.0 / 256, None, op0=ALU.mult),
                sumbanks, [SB("gkc")])
            vop(lambda v: v.tensor_tensor(F("gkc", RSG, HP), F("gkc", MEAN, HP),
                                          F("gkc", MEAN, HP), op=ALU.mult),
                [SB("gkc")], [SB("gkc")])
            vop((lambda: lambda v: v.scalar_tensor_tensor(
                F("gkc", RSTD, HP), pball[:, NBK2 * 512:NBK2 * 512 + HP], 1.0 / 256,
                F("gkc", RSG, HP), op0=ALU.mult, op1=ALU.subtract))(),
                sqbanks + [SB("gkc")], [SB("gkc")])
            vop(lambda sc: sc.activation(F("gkc", RSG, HP), F("gkc", RSTD, HP), AF.Ln, bias=F("eps", 0, 1)),
                [SB("gkc"), SB("eps")], [SB("gkc")], engine="scalar")
            vop(lambda sc: sc.activation(F("gkc", RSTD, HP), F("gkc", RSG, HP), AF.Exp, scale=-0.5),
                [SB("gkc")], [SB("gkc")], engine="scalar")
            og = "og%d" % sl
            for vdc in range(2):
                sil_ap = Bap("silug", vdc * 256 + sl * 128, [[512, H], [1, 128]])
                scr = "scr1" if vdc == 0 else "scr2"
                vop((lambda vdc, scr: lambda v: v.tensor_tensor(
                    F(scr, 0, HP), B("obf", vdc * HP, HP), F("gkc", MEAN, HP),
                    op=ALU.subtract))(vdc, scr),
                    [SB("obf"), SB("gkc")], [SB(scr)])
                vop((lambda sil_ap: lambda v: v.tensor_tensor(
                    F("gkc", RSG, HP), F("gkc", RSTD, HP), sil_ap, op=ALU.mult))(sil_ap),
                    [SB("silug")], [SB("gkc")])
                og_ap = Bap(og, vdc * 128, [[256, H], [1, 128]])
                vop((lambda og_ap, scr: lambda v: v.tensor_tensor(
                    og_ap, F(scr, 0, HP), F("gkc", RSG, HP), op=ALU.mult))(og_ap, scr),
                    [SB(scr), SB("gkc")], [SB(og)])

        def emit_oproj_ln(sci, hc):
            for sl in range(2):
                ch = sci * CPS + 2 * hc + sl
                sload(F("xtmc%d" % sl), d_xtm[ch], SB("xtmc%d" % sl), stream="xtmc%d" % sl)
            for fob in range(FOD):
                for q in range(FC // 4):
                    p = q % 2
                    slot = wp_slot[p]
                    gload("wp%d" % p, B(slot), d_wo[fob, q], SB(slot))
                    for sl in range(2):
                        og = "og%d" % sl
                        bk = 1 + (fob * 2 + sl) % 6
                        for f4 in range(4):
                            oc = 4 * q + f4
                            mm(bank_buf[bk], bank(bk),
                               [SB(slot), SB(og)],
                               B(og, oc * 128, 128),
                               B(slot, f4 * 512, 512),
                               q == 0 and f4 == 0, q == FC // 4 - 1 and f4 == 3,
                               force_tok=(sl == 1 and f4 == 3))
                for sl in range(2):
                    bk = 1 + (fob * 2 + sl) % 6
                    vop((lambda sl, fob, bk: lambda v: v.tensor_tensor(
                        F("rc%d" % sl, fob * 512, 512), F("xtmc%d" % sl, fob * 512, 512),
                        bank(bk), op=ALU.add))(sl, fob, bk),
                        [SB("xtmc%d" % sl), bank_buf[bk]], [SB("rc%d" % sl)])
            for sl in range(2):
                ch = sci * CPS + 2 * hc + sl
                emit_rowln(SB("rc%d" % sl), (lambda sl: lambda lo, w: F("rc%d" % sl, lo, w))(sl),
                           SB("xtmc%d" % sl), (lambda sl: lambda lo, w: F("xtmc%d" % sl, lo, w))(sl),
                           D, SB("x2c"), lambda lo, w: F("x2c", lo, w), "lnv")
                vop(lambda v: v.tensor_copy(B("x2bfc", 0, D), F("x2c", 0, D)),
                    [SB("x2c")], [SB("x2bfc")])
                store(d_x2tm[ch * 128:(ch + 1) * 128, :], SB("x2c"), F("x2c", 0, D), stream="x2tm")
                store(d_x2bf[ch * 128:(ch + 1) * 128, :], SB("x2bfc"), B("x2bfc", 0, D), stream="x2bf")

        def emit_rowln(rbuf, rap, dumpbuf, dumpap, W, obuf, oap, lnv, epsn="eps"):
            vop((lambda: lambda v: v.reduce_sum(F(lnv, 0, 1), rap(0, W), axis=AX.X))(),
                [rbuf], [SB(lnv)])
            vop((lambda: lambda sc: sc.activation(dumpap(0, W), rap(0, W), AF.Square,
                                                  accum_out=F(lnv, 2, 1)))(),
                [rbuf], [dumpbuf, SB(lnv)], engine="scalar")
            vop((lambda: lambda v: v.tensor_scalar(F(lnv, 1, 1), F(lnv, 0, 1),
                                                   1.0 / W, None, op0=ALU.mult))(),
                [SB(lnv)], [SB(lnv)])
            vop((lambda: lambda v: v.tensor_tensor(F(lnv, 3, 1), F(lnv, 1, 1),
                                                   F(lnv, 1, 1), op=ALU.mult))(),
                [SB(lnv)], [SB(lnv)])
            vop((lambda: lambda v: v.scalar_tensor_tensor(
                F(lnv, 4, 1), F(lnv, 2, 1), 1.0 / W, F(lnv, 3, 1),
                op0=ALU.mult, op1=ALU.subtract))(),
                [SB(lnv)], [SB(lnv)])
            vop((lambda epsn: lambda sc: sc.activation(F(lnv, 5, 1), F(lnv, 4, 1), AF.Ln,
                                                        bias=F(epsn, 0, 1)))(epsn),
                [SB(lnv), SB(epsn)], [SB(lnv)], engine="scalar")
            vop((lambda: lambda sc: sc.activation(F(lnv, 6, 1), F(lnv, 5, 1), AF.Exp, scale=-0.5))(),
                [SB(lnv)], [SB(lnv)], engine="scalar")
            vop((lambda: lambda v: v.tensor_scalar(
                oap(0, W), rap(0, W), F(lnv, 1, 1), F(lnv, 6, 1),
                op0=ALU.subtract, op1=ALU.mult))(),
                [rbuf, SB(lnv)], [obuf])

        for sci in range(NSCP):
            emit_attn_superchunk(sci, main=False)
        for sci in range(NSC):
            emit_attn_superchunk(sci, main=True)

        attn_end = P.boundary()
        PHASE["bf"] = bf_ffn
        PHASE["f32"] = f32_ffn
        for name in ("x2sb", "tsb", "wgb0", "wgb1", "wub0", "wub1", "wdb0", "wdb1",
                     "rwtsb_b", "routed", "siluscr0", "siluscr1", "smx0", "smx1", "comb",
                     "r2", "x2tmc", "out2", "lnv2", "eps2", "ygat0", "ygat1", "eqsv",
                     "wsave", "rktm", "iota16", "sprs", "identf", "xtmrow", "keepbf", "idx",
                     "identb2", "ygat0", "ygat1", "ygs0", "ygs1", "ygs2", "ygs3", "onesf2"):
            SB(name).readers = list(attn_end)

        gload("rwt", B("rwtsb", 0, FC * EW), d_rwt, SB("rwtsb_b"))
        vop(lambda v: v.memset(F("eps2", 0, 1), 1e-5), [], [SB("eps2")])
        sload(F("iota16", 0, 16), d_iota16, SB("iota16"), stream="iota")
        sload(F("identf"), d_identf, SB("identf"), stream="identf")
        gload("ident2", B("identb"), d_ident, SB("identb2"))
        vop(lambda v: v.memset(F("onesf2"), 1.0), [], [SB("onesf2")])

        bc_reg = [None]

        def bcreg(gp):
            if bc_reg[0] is None:
                bc_reg[0] = gp.to_reg(E * CC - 1)
            return bc_reg[0]

        def multi_writes(buf, emits):
            base = buf.wr_waits()
            toks = [fn(base) for fn in emits]
            buf.writers = [t for t in toks if t is not None]
            buf.readers = []
            return toks

        def load_x2sb(g):
            g0 = g * G
            def mk(fc):
                def fn(base):
                    return pop("sync", (lambda fc: lambda e: e.dma_start_transpose(
                        out=B("x2sb", fc * G, G),
                        in_=d_x2bf[g0:g0 + G, fc * 128:(fc + 1) * 128]))(fc),
                        waits=list(attn_end) + base, inc=gs("sl_x2sb"), inc_n=16)
                return fn
            multi_writes(SB("x2sb"), [mk(fc) for fc in range(FC)])

        def emit_router_sub(g, sub, bk):
            sp = sub % 2
            so = sp * (6 * 16 + 64)
            SM = lambda i: F("smx", so + i * 16, E)
            SV = lambda i: F("smx", so + 6 * 16 + i, 1)
            SMXB = "smx%d" % sp
            bkb = bank_buf[bk]
            vop((lambda g, sub, bk: lambda sc: sc.activation(
                F("comb", (g * SG + sub) * EW + E, 1), bank(bk, E, 1), AF.Sigmoid))(g, sub, bk),
                [bkb], [SB("comb")], engine="scalar")
            vop((lambda bk, SM: lambda v: v.tensor_copy(SM(0), bank(bk, 0, E)))(bk, SM),
                [bkb], [SB(SMXB)])
            vop((lambda SM, SV: lambda v: v.reduce_max(SV(0), SM(0), axis=AX.X, negate=True))(SM, SV),
                [SB(SMXB)], [SB(SMXB)])
            vop((lambda SM, SV: lambda sc: sc.activation(SM(1), SM(0), AF.Exp, bias=SV(0)))(SM, SV),
                [SB(SMXB)], [SB(SMXB)], engine="scalar")
            vop((lambda SM, SV: lambda v: v.reduce_sum(SV(1), SM(1), axis=AX.X))(SM, SV),
                [SB(SMXB)], [SB(SMXB)])
            vop((lambda SV: lambda v: v.reciprocal(SV(2), SV(1)))(SV),
                [SB(SMXB)], [SB(SMXB)])
            vop((lambda SM, SV: lambda v: v.tensor_scalar(SM(0), SM(1), SV(2), None, op0=ALU.mult))(SM, SV),
                [SB(SMXB)], [SB(SMXB)])
            vop((lambda SM: lambda v: v.tensor_copy(SM(2), SM(0)))(SM), [SB(SMXB)], [SB(SMXB)])
            for it in range(4):
                wsrc, wdst = (2, 3) if it % 2 == 0 else (3, 2)
                vop((lambda wsrc, SM, SV: lambda v: v.reduce_max(SV(3), SM(wsrc), axis=AX.X))(wsrc, SM, SV),
                    [SB(SMXB)], [SB(SMXB)])
                vop((lambda sub, it, SV: lambda v: v.tensor_copy(
                    F("wsave", (g * SG + sub) * 4 + it, 1), SV(3)))(sub, it, SV),
                    [SB(SMXB)], [SB("wsave")])
                vop((lambda wsrc, sub, it, SM, SV: lambda v: v.tensor_scalar(
                    F("eqsv", (sub * 4 + it) * 16, 16, w=E) if False else
                    F("eqsv", (sub * 4 + it) * 16, E),
                    SM(wsrc), SV(3), None, op0=ALU.is_ge))(wsrc, sub, it, SM, SV),
                    [SB(SMXB)], [SB("eqsv")])
                if it < 3:
                    vop((lambda wsrc, wdst, sub, it, SM: lambda v: v.scalar_tensor_tensor(
                        SM(wdst), F("eqsv", (sub * 4 + it) * 16, E), -2.0, SM(wsrc),
                        op0=ALU.mult, op1=ALU.add))(wsrc, wdst, sub, it, SM),
                        [SB(SMXB), SB("eqsv")], [SB(SMXB)])
            vop((lambda SM, SV: lambda v: v.tensor_scalar(SM(4), SM(0), SV(3), None, op0=ALU.is_ge))(SM, SV),
                [SB(SMXB)], [SB(SMXB)])
            vop((lambda SM: lambda v: v.tensor_tensor(SM(1), SM(0), SM(4), op=ALU.mult))(SM),
                [SB(SMXB)], [SB(SMXB)])
            vop((lambda SM, SV: lambda v: v.reduce_sum(SV(4), SM(1), axis=AX.X))(SM, SV),
                [SB(SMXB)], [SB(SMXB)])
            vop((lambda SV: lambda v: v.tensor_scalar(SV(5), SV(4), 1e-9, None, op0=ALU.max))(SV),
                [SB(SMXB)], [SB(SMXB)])
            vop((lambda SV: lambda v: v.reciprocal(SV(6), SV(5)))(SV),
                [SB(SMXB)], [SB(SMXB)])
            for k in range(4):
                vop((lambda sub, k, SV: lambda v: v.tensor_scalar(
                    F("wsave", (g * SG + sub) * 4 + k, 1),
                    F("wsave", (g * SG + sub) * 4 + k, 1), SV(6), None, op0=ALU.mult))(sub, k, SV),
                    [SB(SMXB)], [SB("wsave")])
            vop((lambda SM: lambda v: v.tensor_copy(B("keepbf", 0, E), SM(4)))(SM),
                [SB(SMXB)], [SB("keepbf")])
            mm(bank_buf[0], pbbf[0:E, 0:128],
               [SB("keepbf"), SB("identb2")],
               B("keepbf", 0, E), None, True, True, transpose_in=B("identb"))
            vop((lambda sub: lambda v: v.tensor_copy(
                F("r2", sub * 128, 128, p0=0, pn=E), pbbf[0:E, 0:128]))(sub),
                [bank_buf[0]], [SB("r2")])

        def MW(col, w, pn=16):
            o, _ = f32_ffn["r2"]
            return af32[0:pn, o + col:o + col + w]

        def emit_sparse_prep(g):
            for i in range(SG):
                init = 0.0 if i == 0 else MW(G + i * 128 - 1, 1)
                vop((lambda i, init: lambda v: v.tensor_tensor_scan(
                    MW(G + i * 128, 128),
                    F("onesf2", 0, 128, p0=0, pn=16),
                    MW(i * 128, 128),
                    init, op0=ALU.mult, op1=ALU.add))(i, init),
                    [SB("onesf2"), SB("r2"), SB("x2tmc")], [SB("r2")])
            vop(lambda v: v.tensor_tensor(
                MW(2 * G, G), MW(G, G), MW(0, G), op=ALU.subtract),
                [SB("r2")], [SB("r2"), SB("x2tmc")])
            for sub in range(SG):
                mm(bank_buf[7], bank(7, 0, 16),
                   [SB("r2"), SB("x2tmc"), SB("identf")],
                   MW(2 * G + sub * 128, 128),
                   None, True, True, transpose_in=F("identf", 0, 16, p0=0, pn=16))
                vop(lambda v: v.tensor_copy(F("rktm", 0, 16), bank(7, 0, 16)),
                    [bank_buf[7]], [SB("rktm")])
                sload(B("xtmrow", 0, D), d_x2bf[g * G + sub * 128:g * G + (sub + 1) * 128, :],
                      SB("xtmrow"), stream="xtr")
                for k in range(4):
                    eq = (lambda sub, k: F("eqsv", (sub * 4 + k) * 16, E))(sub, k)
                    vop((lambda sub, k: lambda v: v.tensor_tensor(
                        F("sprs", 0, 16, w=E) if False else F("sprs", 0, E),
                        F("eqsv", (sub * 4 + k) * 16, E), F("iota16", 0, E), op=ALU.mult))(sub, k),
                        [SB("eqsv"), SB("iota16")], [SB("sprs")])
                    vop(lambda v: v.reduce_sum(F("sprs", 16, 1), F("sprs", 0, E), axis=AX.X),
                        [SB("sprs")], [SB("sprs")])
                    vop((lambda sub, k: lambda v: v.tensor_tensor(
                        F("sprs", 0, E), F("eqsv", (sub * 4 + k) * 16, E), F("rktm", 0, E), op=ALU.mult))(sub, k),
                        [SB("eqsv"), SB("rktm")], [SB("sprs")])
                    vop(lambda v: v.reduce_sum(F("sprs", 17, 1), F("sprs", 0, E), axis=AX.X),
                        [SB("sprs")], [SB("sprs")])
                    vop(lambda v: v.scalar_tensor_tensor(
                        F("sprs", 18, 1), F("sprs", 16, 1), float(CC), F("sprs", 17, 1),
                        op0=ALU.mult, op1=ALU.add),
                        [SB("sprs")], [SB("sprs")])
                    vop((lambda g: lambda v: v.tensor_scalar(
                        F("sprs", 19, 1), F("sprs", 18, 1), float(g * C + 1e7), None, op0=ALU.add))(g),
                        [SB("sprs")], [SB("sprs")])
                    vop(lambda v: v.tensor_scalar(
                        F("sprs", 20, 1), F("sprs", 17, 1), float(C) - 0.5, None, op0=ALU.is_le),
                        [SB("sprs")], [SB("sprs")])
                    vop(lambda v: v.scalar_tensor_tensor(
                        F("sprs", 21, 1), F("sprs", 20, 1), -1e7, F("sprs", 19, 1),
                        op0=ALU.mult, op1=ALU.add),
                        [SB("sprs")], [SB("sprs")])
                    col = (g * SG + sub) * 4 + k
                    vop((lambda col: lambda v: v.tensor_copy(idxsb[:, col:col + 1], F("sprs", 21, 1)))(col),
                        [SB("sprs")], [SB("idx")])
                    tok = pop("gpsimd", (lambda col: lambda gp: gp.indirect_dma_start(
                        out=d_xc, out_offset=bass.IndirectOffsetOnAxis(idxsb[:, col:col + 1], 0),
                        in_=B("xtmrow", 0, D), in_offset=None,
                        bounds_check=bcreg(gp), oob_is_err=False))(col),
                        waits=SB("idx").rd_waits() + SB("xtmrow").rd_waits() + [SB("sct%d" % k).writers[-1] if SB("sct%d" % k).writers else None],
                        inc=gs("sct%d" % k), inc_n=16)
                    SB("idx").add_reader(tok)
                    SB("xtmrow").add_reader(tok)
                    SB("sct%d" % k).set_writer(tok)

        def emit_xc_transposes(e):
            def mk(fc):
                def fn(base):
                    return pop("sync", (lambda fc: lambda ee: ee.dma_start_transpose(
                        out=B("x2sb", fc * CC, CC),
                        in_=d_xc[e * CC:(e + 1) * CC, fc * 128:(fc + 1) * 128]))(fc),
                        waits=base + scat_end, inc=gs("sl_xct"), inc_n=16)
                return fn
            multi_writes(SB("x2sb"), [mk(fc) for fc in range(FC)])

        def emit_sparse_expert2(e):
            for ib in range(IB):
                p = ib % 2
                gload("wgb%d" % p, B("wgb%d" % p), d_wgate[e, ib], SB("wgb%d" % p))
                gload("wub%d" % p, B("wub%d" % p), d_wup[e, ib], SB("wub%d" % p))
                gul = {}
                for kind in range(2):
                    wname = ("wgb%d" if kind == 0 else "wub%d") % p
                    for h, (h0, hw) in enumerate(CHALF):
                        bk = 1 + (p * 2 * NCHALF + kind * NCHALF + h) % 7
                        gul[(kind, h)] = bk
                        for fc in range(FC):
                            mm(bank_buf[bk], bank(bk, 0, hw),
                               [SB(wname), SB("x2sb")],
                               B(wname, fc * 128, 128),
                               B("x2sb", fc * CC + h0, hw),
                               fc == 0, fc == FC - 1)
                        if kind == 0:
                            vop((lambda h, hw, bk: lambda sc: sc.activation(
                                F("siluscr", h * 512, hw), bank(bk, 0, hw), AF.Silu))(h, hw, bk),
                                [bank_buf[bk]], [SB("siluscr%d" % h)], engine="scalar")
                for h, (h0, hw) in enumerate(CHALF):
                    ubk = gul[(1, h)]
                    vop((lambda ib, h0, hw, ubk, h: lambda v: v.tensor_tensor(
                        B("tsb", ib * CC + h0, hw), F("siluscr", h * 512, hw),
                        bank(ubk, 0, hw), op=ALU.mult))(ib, h0, hw, ubk, h),
                        [SB("siluscr%d" % h), bank_buf[ubk]], [SB("tsb")])
            if e + 1 < E:
                emit_xc_transposes(e + 1)
            for fo in range(FOD):
                p = fo % 2
                gload("wdb%d" % p, B("wdb%d" % p), d_wdown[e, fo], SB("wdb%d" % p))
                for sub in range(CSUB):
                    bk = 1 + (fo * CSUB + sub) % 7
                    for ib in range(IB):
                        mm(bank_buf[bk], bank(bk),
                           [SB("tsb"), SB("wdb%d" % p)],
                           B("tsb", ib * CC + sub * 128, 128),
                           B("wdb%d" % p, ib * 512, 512),
                           ib == 0, ib == IB - 1)
                    ysl = (sub % 2) * 2 + fo % 2
                    vop((lambda bk, ysl: lambda v: v.tensor_copy(
                        B("ygat", ysl * 512, 512), bank(bk)))(bk, ysl),
                        [bank_buf[bk]], [SB("ygs%d" % ysl)])
                    store(d_yc[e * CC + sub * 128:e * CC + (sub + 1) * 128,
                               fo * 512:(fo + 1) * 512],
                          SB("ygs%d" % ysl), B("ygat", ysl * 512, 512),
                          stream="yc%d" % ysl)

        def seed_ygat():
            for yp in range(2):
                for s4 in range(4):
                    SB("ygat%d" % yp).readers += SB("ygs%d" % s4).readers
                    SB("ygat%d" % yp).writers += SB("ygs%d" % s4).writers

        def init_routed():
            base = SB("routed")
            ws = base.wr_waits()
            toks = []
            for s in range(SG):
                toks.append(pop("vector", (lambda s: lambda v: v.memset(F("routed", s * D, D), 0.0))(s),
                                waits=ws, inc=dve))
            base.writers = toks
            base.readers = []

        def emit_shared(g):
            for ee in range(E, E + SHQ):
                emit_dense_expert(g, ee)

        def emit_backward(g):
            for sub in range(SG):
                for k in range(4):
                    yp = k % 2
                    col = (g * SG + sub) * 4 + k
                    gtok = pop("gpsimd", (lambda col, yp: lambda gp: gp.indirect_dma_start(
                        out=B("ygat", yp * D, D), out_offset=None,
                        in_=d_yc, in_offset=bass.IndirectOffsetOnAxis(idxsb[:, col:col + 1], 0),
                        bounds_check=bcreg(gp), oob_is_err=False))(col, yp),
                        waits=SB("ygat%d" % yp).wr_waits() + SB("idx").rd_waits() + yc_end,
                        inc=gs("gyc%d" % yp), inc_n=16)
                    SB("ygat%d" % yp).set_writer(gtok)
                    SB("idx").add_reader(gtok)
                    vop((lambda sub, k, yp, col: lambda v: v.scalar_tensor_tensor(
                        F("routed", sub * D, D), B("ygat", yp * D, D),
                        F("wsave", col, 1), F("routed", sub * D, D),
                        op0=ALU.mult, op1=ALU.add))(sub, k, yp, col),
                        [SB("ygat%d" % yp), SB("wsave"), SB("routed")], [SB("routed")])

        def emit_combine(g):
            for sub in range(SG):
                ch = g * SG + sub
                sload(F("x2tmc"), d_x2tm[ch * 128:(ch + 1) * 128, :], SB("x2tmc"),
                      extra=attn_end, stream="x2tmc")
                vop((lambda sub: lambda v: v.tensor_tensor(
                    F("r2", 0, D), F("x2tmc", 0, D), F("routed", sub * D, D), op=ALU.add))(sub),
                    [SB("x2tmc"), SB("routed")], [SB("r2")])
                emit_rowln(SB("r2"), lambda lo, w: F("r2", lo, w),
                           SB("x2tmc"), lambda lo, w: F("x2tmc", lo, w),
                           D, SB("out2"), lambda lo, w: F("out2", lo, w), "lnv2", epsn="eps2")
                store(d_out[ch * 128:(ch + 1) * 128, :], SB("out2"), F("out2", 0, D), stream="out")

        def emit_dense_expert(g, ee):
            qq = ee - E
            wg_src = lambda ib: d_shg[qq * IB + ib]
            wu_src = lambda ib: d_shu[qq * IB + ib]
            wd_src = lambda fo: d_shd[qq, fo]
            col = lambda sub: (g * SG + sub) * EW + E
            for ib in range(IB):
                p = ib % 2
                gload("wgb%d" % p, B("wgb%d" % p), wg_src(ib), SB("wgb%d" % p))
                gload("wub%d" % p, B("wub%d" % p), wu_src(ib), SB("wub%d" % p))
                gul = {}
                for kind in range(2):
                    wname = ("wgb%d" if kind == 0 else "wub%d") % p
                    for h, (h0, hw) in enumerate(GHALF):
                        bk = 1 + (p * 2 * NHALF + kind * NHALF + h) % 7
                        gul[(kind, h)] = bk
                        for fc in range(FC):
                            mm(bank_buf[bk], bank(bk, 0, hw),
                               [SB(wname), SB("x2sb")],
                               B(wname, fc * 128, 128),
                               B("x2sb", fc * G + h0, hw),
                               fc == 0, fc == FC - 1)
                        if kind == 0:
                            vop((lambda h, hw, bk: lambda sc: sc.activation(
                                F("siluscr", h * 512, hw), bank(bk, 0, hw), AF.Silu))(h, hw, bk),
                                [bank_buf[bk]], [SB("siluscr%d" % h)], engine="scalar")
                for h, (h0, hw) in enumerate(GHALF):
                    ubk = gul[(1, h)]
                    vop((lambda ib, h0, hw, ubk, h: lambda v: v.tensor_tensor(
                        B("tsb", ib * G + h0, hw), F("siluscr", h * 512, hw),
                        bank(ubk, 0, hw), op=ALU.mult))(ib, h0, hw, ubk, h),
                        [SB("siluscr%d" % h), bank_buf[ubk]], [SB("tsb")])
            for fo in range(FOD):
                p = fo % 2
                gload("wdb%d" % p, B("wdb%d" % p), wd_src(fo), SB("wdb%d" % p))
                for sub in range(SG):
                    bk = 1 + (fo * SG + sub) % 7
                    for ib in range(IB):
                        mm(bank_buf[bk], bank(bk),
                           [SB("tsb"), SB("wdb%d" % p)],
                           B("tsb", ib * G + sub * 128, 128),
                           B("wdb%d" % p, ib * 512, 512),
                           ib == 0, ib == IB - 1)
                    vop((lambda sub, fo, bk, cc: lambda v: v.scalar_tensor_tensor(
                        F("routed", sub * D + fo * 512, 512), bank(bk),
                        F("comb", cc, 1),
                        F("routed", sub * D + fo * 512, 512),
                        op0=ALU.mult, op1=ALU.add))(sub, fo, bk, col(sub)),
                        [bank_buf[bk], SB("comb"), SB("routed")], [SB("routed")])

        def emit_router_group(g):
            load_x2sb(g)
            for sub in range(SG):
                bk = [5, 6][sub % 2]
                for fc in range(FC):
                    mm(bank_buf[bk], bank(bk, 0, EW),
                       [SB("rwtsb_b"), SB("x2sb")],
                       B("x2sb", fc * G + sub * 128, 128),
                       B("rwtsb", fc * EW, EW),
                       fc == 0, fc == FC - 1)
                emit_router_sub(g, sub, bk)

        emit_router_group(0)
        emit_sparse_prep(0)
        emit_router_group(1)
        emit_sparse_prep(1)
        init_routed()
        emit_shared(1)
        scat_end = [Tok(gs("sct%d" % k), gs("sct%d" % k).count) for k in range(4)]
        emit_xc_transposes(0)
        for e in range(E):
            emit_sparse_expert2(e)
        yc_end = [Tok(gs("st_yc%d" % i), gs("st_yc%d" % i).count) for i in range(4)]
        seed_ygat()
        emit_backward(1)
        emit_combine(1)
        load_x2sb(0)
        init_routed()
        emit_shared(0)
        emit_backward(0)
        emit_combine(0)

        final_waits = [Tok(s, s.count) for n, s in gsem.items() if n.startswith("st_")]
        P.op("sync", lambda e: e.nop(), waits=final_waits)
        P.emit(nc, es)
    return nc



def stage_core(x_main, x_prefix, cfg):
    M, D, FC, SC, NSC, NSCP = cfg["M"], cfg["D"], cfg["FC"], cfg["SC"], cfg["NSC"], cfg["NSCP"]
    def fm(x, nsc):
        xt = np.ascontiguousarray(x.T)
        return np.ascontiguousarray(
            xt.reshape(FC, 128, nsc, SC).transpose(2, 1, 0, 3)).reshape(nsc, 128, FC * SC)
    return {
        "xfm": fm(x_main, NSC),
        "xpfm": fm(x_prefix, max(NSCP, 1)),
        "xtm": np.ascontiguousarray(x_main.reshape(cfg["NCH"], 128, D)),
    }


def stage_weights(w, cfg):
    D, FC, H, E, IB, SHQ, FOD = (cfg["D"], cfg["FC"], cfg["H"], cfg["E"], cfg["IB"],
                                 cfg["SHQ"], cfg["FOD"])
    st = {}
    def occhunks(W, nout):
        return np.ascontiguousarray(
            W.reshape(FC, 128, nout, 128).transpose(2, 1, 0, 3)).reshape(nout, 128, FC * 128)
    def quarters(W):
        return np.ascontiguousarray(
            W.reshape(FC // 4, 4, 128, FOD, 512).transpose(3, 0, 2, 1, 4)).reshape(
                FOD, FC // 4, 128, 4 * 512)
    st["wq"] = occhunks(w["Wq"], H)
    st["wk"] = occhunks(w["Wk"], H)
    st["wkg1"] = np.ascontiguousarray(
        w["Wkg1"].reshape(FC, 128, 16).transpose(1, 0, 2)).reshape(128, FC * 16)
    st["wkg2"] = np.ascontiguousarray(w["Wkg2"])
    st["bkg2s"] = np.ascontiguousarray(w["bkg2"].reshape(H, 128).T)
    st["wv"] = quarters(w["Wv"])
    st["wg"] = occhunks(w["Wg"], FC)
    st["bgs"] = np.ascontiguousarray(w["bg"].reshape(FC, 128).T)
    st["wo"] = quarters(w["Wo"])
    rw_sg = np.concatenate([w["router_w"].T, w["sh_gate_w"]], axis=1)
    ew = E + 1
    st["rwt"] = np.ascontiguousarray(
        rw_sg.reshape(FC, 128, ew).transpose(1, 0, 2)).reshape(128, FC * ew)
    st["wgate"] = np.ascontiguousarray(
        w["w_gate"].reshape(E, FC, 128, IB, 128).transpose(0, 3, 2, 1, 4)).reshape(
            E, IB, 128, FC * 128)
    st["wup"] = np.ascontiguousarray(
        w["w_up"].reshape(E, FC, 128, IB, 128).transpose(0, 3, 2, 1, 4)).reshape(
            E, IB, 128, FC * 128)
    st["wdown"] = np.ascontiguousarray(
        w["w_down"].reshape(E, IB, 128, FOD, 512).transpose(0, 3, 2, 1, 4)).reshape(
            E, FOD, 128, IB * 512)
    st["shg"] = np.ascontiguousarray(
        w["sh_gate"].reshape(FC, 128, SHQ * IB, 128).transpose(2, 1, 0, 3)).reshape(
            SHQ * IB, 128, FC * 128)
    st["shu"] = np.ascontiguousarray(
        w["sh_up"].reshape(FC, 128, SHQ * IB, 128).transpose(2, 1, 0, 3)).reshape(
            SHQ * IB, 128, FC * 128)
    st["shd"] = np.ascontiguousarray(
        w["sh_down"].reshape(SHQ, IB, 128, FOD, 512).transpose(0, 3, 2, 1, 4)).reshape(
            SHQ, FOD, 128, IB * 512)
    st["iota16"] = np.ascontiguousarray(np.tile(np.arange(16, dtype=np.float32), (128, 1)))
    st["identf"] = np.eye(128, dtype=np.float32)
    st["masku4"] = np.ascontiguousarray(
        np.tile(np.triu(np.ones((128, 128), np.float32)), (1, 4)))
    st["identbf"] = np.eye(128).astype(BFD)
    return st


def kernel(**inputs):
    cfg = make_cfg(mini=False)
    x = np.asarray(inputs["L0_attn_x"])
    w = {
        "Wq": inputs["L0_attn_Wq"], "Wk": inputs["L0_attn_Wk"],
        "Wkg1": inputs["L0_attn_Wkg1"], "Wkg2": inputs["L0_attn_Wkg2"],
        "bkg2": inputs["L0_attn_bkg2"], "Wv": inputs["L0_attn_Wv"],
        "Wg": inputs["L0_attn_Wg"], "bg": inputs["L0_attn_bg"],
        "Wo": inputs["L0_attn_Wo"], "router_w": inputs["L0_ffn_router_w"],
        "w_gate": inputs["L0_ffn_w_gate"], "w_up": inputs["L0_ffn_w_up"],
        "w_down": inputs["L0_ffn_w_down"], "sh_gate": inputs["L0_ffn_sh_gate"],
        "sh_up": inputs["L0_ffn_sh_up"], "sh_down": inputs["L0_ffn_sh_down"],
        "sh_gate_w": inputs["L0_ffn_sh_gate_w"],
    }
    w = {k: np.asarray(v, np.float32) for k, v in w.items()}
    wst = stage_weights(w, cfg)
    nc = build_program(cfg)
    in_maps = []
    zeros_pfx = np.zeros((cfg["PFX"], cfg["D"]), np.float32)
    for core in range(8):
        b, half = core // 2, core % 2
        xm = np.ascontiguousarray(x[b, half * 2048:(half + 1) * 2048, :], dtype=np.float32)
        xp = (np.ascontiguousarray(x[b, 0:2048, :], dtype=np.float32)
              if half == 1 else zeros_pfx)
        m = dict(wst)
        m.update(stage_core(xm, xp, cfg))
        in_maps.append(m)
    res = run_bass_kernel_spmd(nc, in_maps, core_ids=list(range(8)))
    out = np.empty((4, 4096, 2048), np.float32)
    for core in range(8):
        b, half = core // 2, core % 2
        out[b, half * 2048:(half + 1) * 2048, :] = res.results[core]["out"]
    return out


# version 07
# speedup vs baseline: 1.2199x; 1.2199x over previous
import numpy as np
import ml_dtypes
import concourse.bass as bass
import concourse.mybir as mybir
from concourse.bass_utils import run_bass_kernel_spmd

F32 = mybir.dt.float32
BF16 = mybir.dt.bfloat16
AF = mybir.ActivationFunctionType
ALU = mybir.AluOpType
AX = mybir.AxisListType
BFD = ml_dtypes.bfloat16


class Sem:
    def __init__(self, name):
        self.name = name
        self.count = 0
        self.h = None

class Tok:
    __slots__ = ("sem", "val")
    def __init__(self, sem, val):
        self.sem = sem
        self.val = val

class Plan:
    def __init__(self):
        self.ops = []
        self.sems = {}

    def sem(self, name):
        if name not in self.sems:
            self.sems[name] = Sem(name)
        return self.sems[name]

    def op(self, engine, emit_fn, waits=(), inc=None, inc_n=1):
        waits = [w for w in waits if w is not None]
        tok = None
        incs = []
        if inc is not None:
            inc.count += inc_n
            tok = Tok(inc, inc.count)
            incs.append((inc, inc_n))
        self.ops.append((engine, emit_fn, waits, incs))
        return tok

    def boundary(self):
        return [Tok(s, s.count) for s in self.sems.values()]

    def emit(self, nc, es):
        for s in self.sems.values():
            s.h = es.enter_context(nc.semaphore(s.name))
        blk = es.enter_context(nc.Block())
        by_eng = {}
        for rec in self.ops:
            by_eng.setdefault(rec[0], []).append(rec)

        def run_engine(eng_obj, recs):
            watermark = {}
            for _, emit_fn, waits, incs in recs:
                by_sem = {}
                for w in waits:
                    if by_sem.get(w.sem.name, (None, -1))[1] < w.val:
                        by_sem[w.sem.name] = (w.sem, w.val)
                for name, (sem, val) in by_sem.items():
                    if watermark.get(name, -1) >= val:
                        continue
                    eng_obj.wait_ge(sem.h, val)
                    watermark[name] = val
                instr = emit_fn(eng_obj)
                for s, n in incs:
                    instr.then_inc(s.h, n)

        for name in ("sync", "scalar", "vector", "tensor", "gpsimd"):
            if name not in by_eng:
                continue
            recs = by_eng[name]
            section = getattr(blk, name)
            def make(rs):
                def _f(eng_obj):
                    run_engine(eng_obj, rs)
                return _f
            section(make(recs))
        return blk

class Buf:
    __slots__ = ("writers", "readers")
    def __init__(self):
        self.writers = []
        self.readers = []

    def rd_waits(self):
        return list(self.writers)

    def add_reader(self, tok):
        if tok is not None:
            self.readers.append(tok)

    def wr_waits(self):
        return self.readers + self.writers

    def set_writer(self, tok):
        self.writers = [tok] if tok is not None else []
        self.readers = []

    def add_writer(self, tok):
        if tok is not None:
            self.writers.append(tok)


def make_cfg(mini=False):
    c = {}
    if mini == "midi":
        c.update(D=2048, E=16, IB=2, SHQ=2, M=2048, PFX=512, ROWW=512)
    elif mini:
        c.update(D=512, E=4, IB=2, SHQ=2, M=1024, PFX=512, ROWW=128)
    else:
        c.update(D=2048, E=16, IB=11, SHQ=4, M=2048, PFX=2048, ROWW=512)
    D = c["D"]
    c["FC"] = D // 128
    c["H"] = D // 256
    c["SC"] = 512
    c["CPS"] = 4
    c["NSC"] = c["M"] // c["SC"]
    c["NSCP"] = c["PFX"] // c["SC"]
    c["G"] = c["M"] // 2
    c["SG"] = c["G"] // 128
    c["FOD"] = D // 512
    c["GHALF"] = [(h * 512, min(512, c["G"] - h * 512)) for h in range((c["G"] + 511) // 512)]
    c["NCH"] = c["M"] // 128
    c["C"] = min(384, c["G"])
    if c["E"] <= 4:
        c["C"] = c["G"]
    return c


def build_program(cfg, debug_spill=False):
    D, FC, H, E, IB, SHQ = cfg["D"], cfg["FC"], cfg["H"], cfg["E"], cfg["IB"], cfg["SHQ"]
    M, PFX, SC, G, SG, FOD = cfg["M"], cfg["PFX"], cfg["SC"], cfg["G"], cfg["SG"], cfg["FOD"]
    NSC, NSCP, ROWW, CPS = cfg["NSC"], cfg["NSCP"], cfg["ROWW"], cfg["CPS"]
    GHALF, NCH = cfg["GHALF"], cfg["NCH"]
    NHALF = len(GHALF)
    C = cfg["C"]
    CC = 2 * C
    CHALF = [(h * 512, min(512, CC - h * 512)) for h in range((CC + 511) // 512)]
    NCHALF = len(CHALF)
    CSUB = CC // 128
    EW = E + 1
    HP = H * 128
    NBO = (2 * HP) // 512 if 2 * HP >= 512 else 1
    NAB = (H + 3) // 4
    NP = (HP + ROWW - 1) // ROWW
    SCALING = 0.0625
    S16 = 1.0 / 16.0

    nc = bass.Bass("TRN2", target_bir_lowering=False)

    def din(name, shape, dt=F32):
        return nc.dram_tensor(name, shape, dt, kind="ExternalInput").ap()

    d_xfm = din("xfm", [NSC, 128, FC * SC])
    d_xpfm = din("xpfm", [max(NSCP, 1), 128, FC * SC])
    d_xtm = din("xtm", [NCH, 128, D])
    d_wq = din("wq", [H, 128, FC * 128])
    d_wk = din("wk", [H, 128, FC * 128])
    d_wkg1 = din("wkg1", [128, FC * 16])
    d_wkg2 = din("wkg2", [16, H * 128])
    d_bkg2 = din("bkg2s", [128, H])
    d_wv = din("wv", [FOD, FC // 4, 128, 4 * 512])
    d_wg = din("wg", [FC, 128, FC * 128])
    d_bg = din("bgs", [128, FC])
    d_wo = din("wo", [FOD, FC // 4, 128, 4 * 512])
    d_rwt = din("rwt", [128, FC * EW])
    d_wgate = din("wgate", [E, IB, 128, FC * 128])
    d_wup = din("wup", [E, IB, 128, FC * 128])
    d_wdown = din("wdown", [E, FOD, 128, IB * 512])
    d_shg = din("shg", [SHQ * IB, 128, FC * 128])
    d_shu = din("shu", [SHQ * IB, 128, FC * 128])
    d_shd = din("shd", [SHQ, FOD, 128, IB * 512])
    d_iota16 = din("iota16", [128, 16])
    d_identf = din("identf", [128, 128])
    d_masku = din("masku4", [128, 4 * 128])
    d_ident = din("identbf", [128, 128], BF16)

    d_out = nc.dram_tensor("out", [M, D], F32, kind="ExternalOutput").ap()
    spill_kind = "ExternalOutput" if debug_spill else "Internal"
    d_x2tm = nc.dram_tensor("x2tm_sp", [M, D], F32, kind=spill_kind).ap()
    d_x2bf = nc.dram_tensor("x2bf_sp", [M, D], BF16, kind=spill_kind).ap()
    d_xc = nc.dram_tensor("xc_sp", [E * CC, D], BF16, kind=spill_kind).ap()
    d_yc = nc.dram_tensor("yc_sp", [E * CC, D], BF16, kind=spill_kind).ap()

    from contextlib import ExitStack
    es = ExitStack()
    with es:
        def mklay(pairs):
            lay, off = {}, 0
            for name, w in pairs:
                w = (w + 15) // 16 * 16
                lay[name] = (off, w)
                off += w
            return lay, off

        bf_attn, n1 = mklay([
            ("xsb", FC * SC), ("qd", H * SC), ("kneg", H * SC), ("kdfm", H * SC),
            ("kdtm", H * 128), ("vtm0", D), ("vtm1", D),
            ("silug", FC * 256), ("og0", FC * 128), ("og1", FC * 128),
            ("Asb", H * 128), ("Sbf", H * 256), ("kg1", 512),
            ("wp0", max(FC * 128, 2048)), ("wp1", max(FC * 128, 2048)), ("wp2", max(FC * 128, 2048)), ("wp3", max(FC * 128, 2048)), ("wkg2sb", H * 128),
            ("obf", H * 256), ("sqbf", H * 256), ("rowbf", 2 * HP),
            ("x2bfc", D), ("identb", 128), ("wkg1sb", FC * 16),
        ])
        bf_ffn, n2 = mklay([
            ("x2sb", max(FC * G, FC * CC)), ("tsb", max(IB * G, IB * CC)),
            ("xtmrow", D), ("keepbf", 16), ("ygat", 2 * D), ("identb", 128),
            ("wgb0", FC * 128), ("wgb1", FC * 128),
            ("wub0", FC * 128), ("wub1", FC * 128),
            ("wdb0", IB * 512), ("wdb1", IB * 512),
            ("rwtsb", FC * EW),
        ])
        NBF = max(n1, n2) + 16

        f32_attn, m1 = mklay([
            ("zc", H * 128), ("scr1", H * 128), ("scr2", H * 128),
            ("gkc", max(H * SC, 3 * HP)),
            ("ehs", 2 * SC), ("dl", 64), ("gkcl", 64),
            ("xtmc0", D), ("xtmc1", D), ("rc0", D), ("rc1", D),
            ("x2c", D), ("rows", 2 * HP), ("lnv", 64),
            ("S", H * 256), ("masku", 4 * 128), ("onesf", 128),
            ("bkg2", 16), ("bgc", FC), ("eps", 16),
        ])
        f32_ffn, m2 = mklay([
            ("routed", SG * D), ("siluscr", NHALF * 512), ("smx", 2 * (6 * 16 + 64)),
            ("comb", 2 * SG * EW), ("r2", D), ("x2tmc", D), ("out2", D), ("lnv2", 64),
            ("eps2", 16), ("eqsv", SG * 4 * 16), ("wsave", 2 * SG * 4),
            ("rktm", 32), ("iota16", 16), ("sprs", 32), ("identf", 128), ("onesf2", 128),
        ])
        NF32 = max(m1, m2) + 16

        abf = es.enter_context(nc.sbuf_tensor("abf", [128, NBF], BF16))
        af32 = es.enter_context(nc.sbuf_tensor("af32", [128, NF32], F32))
        onesbc = es.enter_context(nc.sbuf_tensor("onesbc", [128, 1], BF16))
        U32 = mybir.dt.uint32
        idxsb = es.enter_context(nc.sbuf_tensor("idxsb", [128, 2 * SG * 4], U32))
        onesbr = es.enter_context(nc.sbuf_tensor("onesbr", [1, 128], BF16))

        PHASE = {"bf": bf_attn, "f32": f32_attn}

        def B(name, lo=0, w=None, p0=0, pn=128):
            o, ww = PHASE["bf"][name]
            if w is None:
                w = ww - lo
            assert lo + w <= ww, (name, lo, w, ww)
            return abf[p0:p0 + pn, o + lo: o + lo + w]

        def F(name, lo=0, w=None, p0=0, pn=128):
            o, ww = PHASE["f32"][name]
            if w is None:
                w = ww - lo
            assert lo + w <= ww, (name, lo, w, ww)
            return af32[p0:p0 + pn, o + lo: o + lo + w]

        def Fap(name, lo, dims):
            o, _ = PHASE["f32"][name]
            return bass.AP(af32, o + lo, [[NF32, 128]] + dims)

        def Bap(name, lo, dims):
            o, _ = PHASE["bf"][name]
            return bass.AP(abf, o + lo, [[NBF, 128]] + dims)

        pbbf = es.enter_context(nc.psum_tensor("pbbf", [128, 1024], BF16))
        pball = es.enter_context(nc.psum_tensor("pball", [128, 3584], F32))

        def bank(i, lo=0, w=512, p0=0, pn=128):
            assert 1 <= i <= 7 and lo + w <= 512
            return pball[p0:p0 + pn, (i - 1) * 512 + lo:(i - 1) * 512 + lo + w]

        P = Plan()
        pes = P.sem("pe")
        dve = P.sem("dve")
        act = P.sem("act")
        pool = P.sem("pool")
        sts = P.sem("st")
        lds = P.sem("lds")
        gsem = {}
        def gs(name):
            if name not in gsem:
                gsem[name] = P.sem("g_" + name)
            return gsem[name]

        bank_buf = {i: Buf() for i in range(8)}
        bufs = {}
        def SB(name):
            if name not in bufs:
                bufs[name] = Buf()
            return bufs[name]

        def pop(engine, emit_fn, waits=(), inc=None, inc_n=1):
            b_, f_ = PHASE["bf"], PHASE["f32"]
            def wrapped(e, emit_fn=emit_fn, b_=b_, f_=f_):
                PHASE["bf"], PHASE["f32"] = b_, f_
                return emit_fn(e)
            return P.op(engine, wrapped, waits=waits, inc=inc, inc_n=inc_n)

        def gload(stream, dst_ap, src_ap, buf, multi=False):
            tok = pop("gpsimd", (lambda d, s: lambda g: g.dma_start(out=d, in_=s))(dst_ap, src_ap),
                       waits=buf.wr_waits(), inc=gs(stream), inc_n=16)
            if multi:
                buf.add_writer(tok)
            else:
                buf.set_writer(tok)
            return tok

        def sload(dst_ap, src_ap, buf, transpose=False, extra=(), multi=False, stream="s"):
            if transpose:
                fn = (lambda d, s: lambda e: e.dma_start_transpose(out=d, in_=s))(dst_ap, src_ap)
            else:
                fn = (lambda d, s: lambda e: e.dma_start(out=d, in_=s))(dst_ap, src_ap)
            tok = pop("sync", fn, waits=list(extra) + (buf.writers if multi else buf.wr_waits()),
                       inc=gs("sl_" + stream), inc_n=16)
            if multi:
                buf.add_writer(tok)
            else:
                buf.set_writer(tok)
            return tok

        def store(dst_ap, src_buf, src_ap, stream="o"):
            tok = pop("sync", (lambda d, s: lambda e: e.dma_start(out=d, in_=s))(dst_ap, src_ap),
                       waits=src_buf.rd_waits(), inc=gs("st_" + stream), inc_n=16)
            src_buf.add_reader(tok)
            return tok

        open_grp = {}

        def mm(bk_buf, out_ap, in_bufs, lhsT, rhs, first, last, transpose_in=None,
               force_tok=False):
            waits = []
            if first:
                waits += bk_buf.wr_waits()
            for b in in_bufs:
                waits += b.rd_waits()
            if transpose_in is not None:
                fn = (lambda o, l, i: lambda t: t.transpose(o, l, i))(out_ap, lhsT, transpose_in)
            else:
                fn = (lambda o, l, r, f, s: lambda t: t.matmul(o, l, r, start=f, stop=s))(
                    out_ap, lhsT, rhs, first, last)
            tok = pop("tensor", fn, waits=waits, inc=(pes if (last or force_tok) else None))
            key = id(bk_buf)
            if first:
                open_grp[key] = []
            if tok is not None:
                for b in in_bufs:
                    b.add_reader(tok)
            else:
                open_grp.setdefault(key, []).extend(in_bufs)
            if last:
                grp = open_grp.pop(key, [])
                seen = set()
                for b in grp:
                    if id(b) not in seen:
                        seen.add(id(b))
                        b.add_reader(tok)
                bk_buf.set_writer(tok)
            return tok

        def vop(fn, reads, writes, waits=(), engine="vector", multi_write=False):
            sem = act if engine == "scalar" else (pool if engine == "gpsimd" else dve)
            ws = list(waits)
            for b in reads:
                ws += b.rd_waits()
            for b in writes:
                ws += (b.writers if multi_write else b.wr_waits())
            tok = pop(engine, fn, waits=ws, inc=sem)
            for b in reads:
                b.add_reader(tok)
            for b in writes:
                if multi_write:
                    b.add_writer(tok)
                else:
                    b.set_writer(tok)
            return tok

        vop(lambda v: v.memset(onesbc[:], 1.0), [], [SB("onesbc")])
        vop(lambda v: v.memset(onesbr[:], 1.0), [], [SB("onesbr")])
        vop(lambda v: v.memset(F("onesf"), 1.0), [], [SB("onesf")])
        vop(lambda v: v.memset(F("S"), 0.0), [], [SB("S")])
        sload(F("masku"), d_masku, SB("masku"), stream="masku")
        sload(F("bkg2", 0, H), d_bkg2, SB("bkg2"), stream="bkg2")
        sload(F("bgc", 0, FC), d_bg, SB("bgc"), stream="bgc")
        vop(lambda v: v.memset(F("eps", 0, 1), 1e-5), [], [SB("eps")])
        gload("ident", B("identb"), d_ident, SB("identb"))
        gload("wkg1", B("wkg1sb", 0, FC * 16), d_wkg1, SB("wkg1"))
        gload("wkg2", B("wkg2sb", p0=0, pn=16), d_wkg2, SB("wkg2"))

        wp_slot = ["wp0", "wp1", "wp2", "wp3"]

        NQ = FC // 4

        def emit_attn_superchunk(sci, main):
            x_src = d_xfm[sci] if main else d_xpfm[sci]
            for xq in range(NQ):
                gload("xsb%d" % xq, B("xsb", xq * 4 * SC, 4 * SC),
                      x_src[:, xq * 4 * SC:(xq + 1) * 4 * SC], SB("xsb%d" % xq))

            for fc in range(FC):
                mm(bank_buf[7], bank(7, p0=0, pn=16, w=SC),
                   [SB("wkg1"), SB("xsb%d" % (fc // 4))],
                   B("wkg1sb", fc * 16, 16),
                   B("xsb", fc * SC, SC),
                   fc == 0, fc == FC - 1)
            vop(lambda v: v.tensor_copy(B("kg1", 0, SC, p0=0, pn=16), bank(7, p0=0, pn=16, w=SC)),
                [bank_buf[7]], [SB("kg1")])

            for c in range(CPS):
                bks = [1, 2] if c % 2 == 0 else [3, 4]
                for h in range(H):
                    bk = bks[h // 4]
                    q4 = h % 4
                    nh = min(4, H - (h // 4) * 4)
                    mm(bank_buf[bk], bank(bk, q4 * 128, 128),
                       [SB("wkg2"), SB("kg1")],
                       B("wkg2sb", h * 128, 128, p0=0, pn=16),
                       B("kg1", c * 128, 128, p0=0, pn=16),
                       q4 == 0, q4 == nh - 1)
                for h in range(H):
                    bk = bks[h // 4]
                    vop((lambda h, bk: lambda v: v.tensor_scalar(
                        F("zc", h * 128, 128), bank(bk, (h % 4) * 128, 128),
                        F("bkg2", h, 1), None, op0=ALU.add))(h, bk),
                        [bank_buf[bk], SB("bkg2")], [SB("zc")])
                vop(lambda sc: sc.activation(F("scr1", 0, HP), F("zc", 0, HP), AF.Abs),
                    [SB("zc")], [SB("scr1")], engine="scalar")
                vop(lambda sc: sc.activation(F("scr2", 0, HP), F("scr1", 0, HP), AF.Exp, scale=-1.0),
                    [SB("scr1")], [SB("scr2")], engine="scalar")
                vop(lambda sc: sc.activation(F("scr1", 0, HP), F("scr2", 0, HP), AF.Ln, bias=1.0),
                    [SB("scr2")], [SB("scr1")], engine="scalar")
                vop(lambda v: v.scalar_tensor_tensor(
                    F("scr2", 0, HP), F("zc", 0, HP), 0.0, F("scr1", 0, HP),
                    op0=ALU.min, op1=ALU.subtract),
                    [SB("zc"), SB("scr1")], [SB("scr2")])
                for h in range(H):
                    vop((lambda h, c: lambda v: v.tensor_tensor_scan(
                        F("gkc", h * SC + c * 128, 128), F("onesf"),
                        F("scr2", h * 128, 128), 0.0, op0=ALU.mult, op1=ALU.add))(h, c),
                        [SB("onesf"), SB("scr2")], [SB("gkc")])

            last_ap = Fap("gkc", 127, [[SC, H], [128, CPS]])
            vop(lambda sc: sc.activation(F("dl", 0, H * CPS), last_ap, AF.Exp, scale=S16),
                [SB("gkc")], [SB("dl")], engine="scalar")
            vop(lambda v: v.tensor_scalar(F("gkcl", 0, H * CPS), last_ap, S16, None, op0=ALU.mult),
                [SB("gkc")], [SB("gkcl")])

            if main:
                for oc in range(H):
                    p = oc % 4
                    slot = wp_slot[p]
                    gload("wp%d" % p, B(slot, 0, FC * 128), d_wq[oc], SB(slot))
                    bk = [5, 6, 7][oc % 3]
                    for fc in range(FC):
                        mm(bank_buf[bk], bank(bk),
                           [SB(slot), SB("xsb%d" % (fc // 4))],
                           B(slot, fc * 128, 128),
                           B("xsb", fc * SC, SC),
                           fc == 0, fc == FC - 1)
                    ep = oc % 2
                    vop((lambda oc, ep: lambda sc: sc.activation(
                        F("ehs", ep * SC, SC), F("gkc", oc * SC, SC), AF.Exp, scale=S16))(oc, ep),
                        [SB("gkc")], [SB("ehs%d" % ep)], engine="scalar")
                    vop((lambda oc, bk, ep: lambda v: v.tensor_tensor(
                        B("qd", oc * SC, SC), F("ehs", ep * SC, SC), bank(bk), op=ALU.mult))(oc, bk, ep),
                        [SB("ehs%d" % ep), bank_buf[bk]], [SB("qd")])

            for oc in range(H):
                p = oc % 4
                slot = wp_slot[p]
                gload("wp%d" % p, B(slot, 0, FC * 128), d_wk[oc], SB(slot))
                bk = [5, 6, 7][oc % 3] if not main else [1, 2, 3][oc % 3]
                for fc in range(FC):
                    mm(bank_buf[bk], bank(bk),
                       [SB(slot), SB("xsb%d" % (fc // 4))],
                       B(slot, fc * 128, 128),
                       B("xsb", fc * SC, SC),
                       fc == 0, fc == FC - 1)
                ep = oc % 2
                if main:
                    vop((lambda oc, ep: lambda sc: sc.activation(
                        F("ehs", ep * SC, SC), F("gkc", oc * SC, SC), AF.Exp, scale=-S16))(oc, ep),
                        [SB("gkc")], [SB("ehs%d" % ep)], engine="scalar")
                    vop((lambda oc, bk, ep: lambda v: v.scalar_tensor_tensor(
                        B("kneg", oc * SC, SC), bank(bk), SCALING, F("ehs", ep * SC, SC),
                        op0=ALU.mult, op1=ALU.mult))(oc, bk, ep),
                        [SB("ehs%d" % ep), bank_buf[bk]], [SB("kneg")])
                for c in range(CPS):
                    vop((lambda oc, c, ep: lambda sc: sc.activation(
                        F("ehs", ep * SC + c * 128, 128), F("gkc", oc * SC + c * 128, 128), AF.Exp,
                        scale=-S16, bias=F("gkcl", oc * CPS + c, 1)))(oc, c, ep),
                        [SB("gkc"), SB("gkcl")], [SB("ehs%d" % ep)], engine="scalar")
                vop((lambda oc, bk, ep: lambda v: v.scalar_tensor_tensor(
                    B("kdfm", oc * SC, SC), bank(bk), SCALING, F("ehs", ep * SC, SC),
                    op0=ALU.mult, op1=ALU.mult))(oc, bk, ep),
                    [SB("ehs%d" % ep), bank_buf[bk]], [SB("kdfm")])

            def emit_g(hc):
                c0 = 2 * hc
                if main:
                    for oc in range(FC):
                        p = oc % 4
                        slot = wp_slot[p]
                        gload("wp%d" % p, B(slot, 0, FC * 128), d_wg[oc], SB(slot))
                        bk = [5, 6, 7][oc % 3]
                        for fc in range(FC):
                            mm(bank_buf[bk], bank(bk, 0, 256),
                               [SB(slot), SB("xsb%d" % (fc // 4))],
                               B(slot, fc * 128, 128),
                               B("xsb", fc * SC + c0 * 128, 256),
                               fc == 0, fc == FC - 1)
                        vop((lambda oc, bk: lambda sc: sc.activation(
                            B("silug", oc * 256, 256), bank(bk, 0, 256), AF.Silu,
                            bias=F("bgc", oc, 1)))(oc, bk),
                            [bank_buf[bk], SB("bgc")], [SB("silug")], engine="scalar")
            def emit_v(hc):
                c0 = 2 * hc
                for vb in range(FOD):
                    for q in range(FC // 4):
                        p = (vb * (FC // 4) + q) % 4
                        slot = wp_slot[p]
                        gload("wp%d" % p, B(slot), d_wv[vb, q], SB(slot))
                        for sl in range(2):
                            c = c0 + sl
                            bk = 1 + ((vb % 2) * 2 + sl)
                            for f4 in range(4):
                                fc = 4 * q + f4
                                mm(bank_buf[bk], bank(bk),
                                   [SB(slot), SB("xsb%d" % (fc // 4))],
                                   B("xsb", fc * SC + c * 128, 128),
                                   B(slot, f4 * 512, 512),
                                   q == 0 and f4 == 0, q == FC // 4 - 1 and f4 == 3,
                                   force_tok=(sl == 1 and f4 == 3))
                    for sl in range(2):
                        bk = 1 + ((vb % 2) * 2 + sl)
                        vop((lambda sl, vb, bk: lambda v: v.tensor_copy(
                            B("vtm%d" % sl, vb * 512, 512), bank(bk)))(sl, vb, bk),
                            [bank_buf[bk]], [SB("vtm%d" % sl)])


            emit_g(0)
            emit_v(0)
            for sl in range(2):
                emit_gla_chunk(sci, 0, sl, main)
            emit_v(1)
            if main:
                emit_oproj_ln(sci, 0)
            emit_g(1)
            for sl in range(2):
                emit_gla_chunk(sci, 1, sl, main)
            if main:
                emit_oproj_ln(sci, 1)

        def emit_gla_chunk(sci, hc, sl, main):
            c = 2 * hc + sl
            vtm = "vtm%d" % sl
            for h in range(H):
                mm(bank_buf[0], pbbf[:, h * 128:(h + 1) * 128],
                   [SB("kdfm"), SB("identb")],
                   B("kdfm", h * SC + c * 128, 128),
                   None, h == 0, h == H - 1, transpose_in=B("identb"))
            vop(lambda v: v.tensor_copy(B("kdtm", 0, H * 128), pbbf[:, 0:H * 128]),
                [bank_buf[0]], [SB("kdtm")])

            if main:
                vop(lambda v: v.tensor_copy(B("Sbf"), F("S")),
                    [SB("S")], [SB("Sbf")])
                for h in range(H):
                    bk = 5 + h // 4
                    q4 = h % 4
                    nh = min(4, H - (h // 4) * 4)
                    mm(bank_buf[bk], bank(bk, q4 * 128, 128),
                       [SB("kneg"), SB("qd")],
                       B("kneg", h * SC + c * 128, 128),
                       B("qd", h * SC + c * 128, 128),
                       q4 == 0, q4 == nh - 1)
                for ab in range(NAB):
                    nh = min(4, H - ab * 4)
                    vop((lambda ab, nh: lambda v: v.tensor_tensor(
                        B("Asb", ab * 512, nh * 128), bank(5 + ab, 0, nh * 128),
                        F("masku", 0, nh * 128), op=ALU.mult))(ab, nh),
                        [bank_buf[5 + ab], SB("masku")], [SB("Asb")])
                cnt = {}
                total = {}
                for h in range(H):
                    for vdc in range(2):
                        bk = 1 + (vdc * HP + h * 128) // 512
                        total[bk] = total.get(bk, 0) + 2
                for h in range(H):
                    for vdc in range(2):
                        bk = 1 + (vdc * HP + h * 128) // 512
                        lo = (vdc * HP + h * 128) % 512
                        n = cnt.get(bk, 0)
                        mm(bank_buf[bk], bank(bk, lo, 128),
                           [SB(vtm), SB("Asb")],
                           B(vtm, h * 256 + vdc * 128, 128),
                           B("Asb", h * 128, 128),
                           n == 0, n + 1 == total[bk])
                        cnt[bk] = n + 1
                        n = cnt[bk]
                        mm(bank_buf[bk], bank(bk, lo, 128),
                           [SB("Sbf"), SB("qd")],
                           B("Sbf", h * 256 + vdc * 128, 128),
                           B("qd", h * SC + c * 128, 128),
                           n == 0, n + 1 == total[bk])
                        cnt[bk] = n + 1
            for h in range(H):
                bk = 7 if h % 2 == 0 else 6
                mm(bank_buf[bk], bank(bk, 0, 256),
                   [SB("kdtm"), SB(vtm)],
                   B("kdtm", h * 128, 128),
                   B(vtm, h * 256, 256),
                   True, True)
                vop((lambda h, c, bk: lambda v: v.scalar_tensor_tensor(
                    F("S", h * 256, 256), F("S", h * 256, 256),
                    F("dl", h * CPS + c, 1), bank(bk, 0, 256),
                    op0=ALU.mult, op1=ALU.add))(h, c, bk),
                    [SB("dl"), bank_buf[bk]], [SB("S")])

            if not main:
                return

            vop(lambda v: v.tensor_copy(B("obf"), pball[:, 0:2 * HP]),
                [bank_buf[b] for b in range(1, 1 + NBO)], [SB("obf")])
            vop(lambda sc: sc.activation(B("sqbf"), B("obf"), AF.Square),
                [SB("obf")], [SB("sqbf")], engine="scalar")
            NBK2 = (HP + 511) // 512
            rowmm = []
            kindless_off = {}
            for kind in range(2):
                src_n = "obf" if kind == 0 else "sqbf"
                for part in range(NP):
                    w = min(ROWW, HP - part * ROWW)
                    bk = 1 + kind * NBK2 + (part * ROWW) // 512
                    lo = (part * ROWW) % 512
                    kindless_off[(bk, lo)] = part * ROWW
                    for vdc in range(2):
                        rowmm.append((bk, lo, w, src_n, vdc))
            per_bank = {}
            for bk, *_ in rowmm:
                per_bank[bk] = per_bank.get(bk, 0) + 1
            seen = {}
            for bk, lo, w, src_n, vdc in rowmm:
                n = seen.get(bk, 0)
                part_off = kindless_off[(bk, lo)]
                mm(bank_buf[bk], bank(bk, lo, w, p0=0, pn=1),
                   [SB("onesbc"), SB(src_n)],
                   onesbc[:],
                   B(src_n, vdc * HP + part_off, w),
                   n == 0, n + 1 == per_bank[bk])
                seen[bk] = n + 1
            vop(lambda v: v.tensor_copy(B("rowbf", 0, 2 * HP, p0=0, pn=1),
                                        pball[0:1, 0:2 * HP]),
                [bank_buf[b] for b in range(1, 1 + 2 * NBK2)], [SB("rowbf")])
            for kind in range(2):
                for part in range(NP):
                    w = min(ROWW, HP - part * ROWW)
                    bk = 1 + kind * NBK2 + (part * ROWW) // 512
                    lo = (part * ROWW) % 512
                    first = (part * ROWW) % 512 == 0
                    last = (part == NP - 1) or ((part + 1) * ROWW) % 512 == 0
                    mm(bank_buf[bk], bank(bk, lo, w),
                       [SB("onesbr"), SB("rowbf")],
                       onesbr[:],
                       B("rowbf", kind * HP + part * ROWW, w, p0=0, pn=1),
                       first, last)
            MEAN, RSTD, RSG = 0, HP, 2 * HP
            sumbanks = [bank_buf[b] for b in range(1, 1 + NBK2)]
            sqbanks = [bank_buf[b] for b in range(1 + NBK2, 1 + 2 * NBK2)]
            vop(lambda v: v.tensor_scalar(F("gkc", MEAN, HP), pball[:, 0:HP],
                                          1.0 / 256, None, op0=ALU.mult),
                sumbanks, [SB("gkc")])
            vop(lambda v: v.tensor_tensor(F("gkc", RSG, HP), F("gkc", MEAN, HP),
                                          F("gkc", MEAN, HP), op=ALU.mult),
                [SB("gkc")], [SB("gkc")])
            vop((lambda: lambda v: v.scalar_tensor_tensor(
                F("gkc", RSTD, HP), pball[:, NBK2 * 512:NBK2 * 512 + HP], 1.0 / 256,
                F("gkc", RSG, HP), op0=ALU.mult, op1=ALU.subtract))(),
                sqbanks + [SB("gkc")], [SB("gkc")])
            vop(lambda sc: sc.activation(F("gkc", RSG, HP), F("gkc", RSTD, HP), AF.Ln, bias=F("eps", 0, 1)),
                [SB("gkc"), SB("eps")], [SB("gkc")], engine="scalar")
            vop(lambda sc: sc.activation(F("gkc", RSTD, HP), F("gkc", RSG, HP), AF.Exp, scale=-0.5),
                [SB("gkc")], [SB("gkc")], engine="scalar")
            og = "og%d" % sl
            for vdc in range(2):
                sil_ap = Bap("silug", vdc * 256 + sl * 128, [[512, H], [1, 128]])
                scr = "scr1" if vdc == 0 else "scr2"
                vop((lambda vdc, scr: lambda v: v.tensor_tensor(
                    F(scr, 0, HP), B("obf", vdc * HP, HP), F("gkc", MEAN, HP),
                    op=ALU.subtract))(vdc, scr),
                    [SB("obf"), SB("gkc")], [SB(scr)], engine="gpsimd")
                vop((lambda sil_ap: lambda v: v.tensor_tensor(
                    F("gkc", RSG, HP), F("gkc", RSTD, HP), sil_ap, op=ALU.mult))(sil_ap),
                    [SB("silug")], [SB("gkc")], engine="gpsimd")
                og_ap = Bap(og, vdc * 128, [[256, H], [1, 128]])
                vop((lambda og_ap, scr: lambda v: v.tensor_tensor(
                    og_ap, F(scr, 0, HP), F("gkc", RSG, HP), op=ALU.mult))(og_ap, scr),
                    [SB(scr), SB("gkc")], [SB(og)], engine="gpsimd")

        def emit_oproj_ln(sci, hc):
            for sl in range(2):
                ch = sci * CPS + 2 * hc + sl
                sload(F("xtmc%d" % sl), d_xtm[ch], SB("xtmc%d" % sl), stream="xtmc%d" % sl)
            for fob in range(FOD):
                for q in range(FC // 4):
                    p = (fob * (FC // 4) + q) % 4
                    slot = wp_slot[p]
                    gload("wp%d" % p, B(slot), d_wo[fob, q], SB(slot))
                    for sl in range(2):
                        og = "og%d" % sl
                        bk = 1 + (fob * 2 + sl) % 6
                        for f4 in range(4):
                            oc = 4 * q + f4
                            mm(bank_buf[bk], bank(bk),
                               [SB(slot), SB(og)],
                               B(og, oc * 128, 128),
                               B(slot, f4 * 512, 512),
                               q == 0 and f4 == 0, q == FC // 4 - 1 and f4 == 3,
                               force_tok=(sl == 1 and f4 == 3))
                for sl in range(2):
                    bk = 1 + (fob * 2 + sl) % 6
                    vop((lambda sl, fob, bk: lambda v: v.tensor_tensor(
                        F("rc%d" % sl, fob * 512, 512), F("xtmc%d" % sl, fob * 512, 512),
                        bank(bk), op=ALU.add))(sl, fob, bk),
                        [SB("xtmc%d" % sl), bank_buf[bk]], [SB("rc%d" % sl)])
            for sl in range(2):
                ch = sci * CPS + 2 * hc + sl
                emit_rowln(SB("rc%d" % sl), (lambda sl: lambda lo, w: F("rc%d" % sl, lo, w))(sl),
                           SB("xtmc%d" % sl), (lambda sl: lambda lo, w: F("xtmc%d" % sl, lo, w))(sl),
                           D, SB("x2c"), lambda lo, w: F("x2c", lo, w), "lnv")
                vop(lambda v: v.tensor_copy(B("x2bfc", 0, D), F("x2c", 0, D)),
                    [SB("x2c")], [SB("x2bfc")])
                store(d_x2tm[ch * 128:(ch + 1) * 128, :], SB("x2c"), F("x2c", 0, D), stream="x2tm")
                store(d_x2bf[ch * 128:(ch + 1) * 128, :], SB("x2bfc"), B("x2bfc", 0, D), stream="x2bf")

        def emit_rowln(rbuf, rap, dumpbuf, dumpap, W, obuf, oap, lnv, epsn="eps"):
            vop((lambda: lambda v: v.reduce_sum(F(lnv, 0, 1), rap(0, W), axis=AX.X))(),
                [rbuf], [SB(lnv)])
            vop((lambda: lambda sc: sc.activation(dumpap(0, W), rap(0, W), AF.Square,
                                                  accum_out=F(lnv, 2, 1)))(),
                [rbuf], [dumpbuf, SB(lnv)], engine="scalar")
            vop((lambda: lambda v: v.tensor_scalar(F(lnv, 1, 1), F(lnv, 0, 1),
                                                   1.0 / W, None, op0=ALU.mult))(),
                [SB(lnv)], [SB(lnv)])
            vop((lambda: lambda v: v.tensor_tensor(F(lnv, 3, 1), F(lnv, 1, 1),
                                                   F(lnv, 1, 1), op=ALU.mult))(),
                [SB(lnv)], [SB(lnv)])
            vop((lambda: lambda v: v.scalar_tensor_tensor(
                F(lnv, 4, 1), F(lnv, 2, 1), 1.0 / W, F(lnv, 3, 1),
                op0=ALU.mult, op1=ALU.subtract))(),
                [SB(lnv)], [SB(lnv)])
            vop((lambda epsn: lambda sc: sc.activation(F(lnv, 5, 1), F(lnv, 4, 1), AF.Ln,
                                                        bias=F(epsn, 0, 1)))(epsn),
                [SB(lnv), SB(epsn)], [SB(lnv)], engine="scalar")
            vop((lambda: lambda sc: sc.activation(F(lnv, 6, 1), F(lnv, 5, 1), AF.Exp, scale=-0.5))(),
                [SB(lnv)], [SB(lnv)], engine="scalar")
            vop((lambda: lambda v: v.tensor_scalar(
                oap(0, W), rap(0, W), F(lnv, 1, 1), F(lnv, 6, 1),
                op0=ALU.subtract, op1=ALU.mult))(),
                [rbuf, SB(lnv)], [obuf])

        for sci in range(NSCP):
            emit_attn_superchunk(sci, main=False)
        for sci in range(NSC):
            emit_attn_superchunk(sci, main=True)

        attn_end = P.boundary()
        PHASE["bf"] = bf_ffn
        PHASE["f32"] = f32_ffn
        for name in ("x2sb", "tsb", "wgb0", "wgb1", "wub0", "wub1", "wdb0", "wdb1",
                     "rwtsb_b", "routed", "siluscr0", "siluscr1", "smx0", "smx1", "comb",
                     "r2", "x2tmc", "out2", "lnv2", "eps2", "ygat0", "ygat1", "eqsv",
                     "wsave", "rktm", "iota16", "sprs", "identf", "xtmrow", "keepbf", "idx",
                     "identb2", "ygat0", "ygat1", "ygs0", "ygs1", "ygs2", "ygs3", "onesf2"):
            SB(name).readers = list(attn_end)

        gload("rwt", B("rwtsb", 0, FC * EW), d_rwt, SB("rwtsb_b"))
        vop(lambda v: v.memset(F("eps2", 0, 1), 1e-5), [], [SB("eps2")])
        sload(F("iota16", 0, 16), d_iota16, SB("iota16"), stream="iota")
        sload(F("identf"), d_identf, SB("identf"), stream="identf")
        gload("ident2", B("identb"), d_ident, SB("identb2"))
        vop(lambda v: v.memset(F("onesf2"), 1.0), [], [SB("onesf2")])

        bc_reg = [None]

        def bcreg(gp):
            if bc_reg[0] is None:
                bc_reg[0] = gp.to_reg(E * CC - 1)
            return bc_reg[0]

        def multi_writes(buf, emits):
            base = buf.wr_waits()
            toks = [fn(base) for fn in emits]
            buf.writers = [t for t in toks if t is not None]
            buf.readers = []
            return toks

        def load_x2sb(g):
            g0 = g * G
            def mk(fc):
                def fn(base):
                    return pop("sync", (lambda fc: lambda e: e.dma_start_transpose(
                        out=B("x2sb", fc * G, G),
                        in_=d_x2bf[g0:g0 + G, fc * 128:(fc + 1) * 128]))(fc),
                        waits=list(attn_end) + base, inc=gs("sl_x2sb"), inc_n=16)
                return fn
            multi_writes(SB("x2sb"), [mk(fc) for fc in range(FC)])

        def emit_router_sub(g, sub, bk):
            sp = sub % 2
            so = sp * (6 * 16 + 64)
            SM = lambda i: F("smx", so + i * 16, E)
            SV = lambda i: F("smx", so + 6 * 16 + i, 1)
            SMXB = "smx%d" % sp
            bkb = bank_buf[bk]
            vop((lambda SV, bk: lambda sc: sc.activation(
                SV(7), bank(bk, E, 1), AF.Exp, scale=-1.0))(SV, bk),
                [bkb], [SB(SMXB)], engine="scalar")
            vop((lambda SV: lambda v: v.tensor_scalar(SV(7), SV(7), 1.0, None, op0=ALU.add))(SV),
                [SB(SMXB)], [SB(SMXB)])
            vop((lambda g, sub, SV: lambda v: v.reciprocal(
                F("comb", (g * SG + sub) * EW + E, 1), SV(7)))(g, sub, SV),
                [SB(SMXB)], [SB("comb")])
            vop((lambda bk, SM: lambda v: v.tensor_copy(SM(0), bank(bk, 0, E)))(bk, SM),
                [bkb], [SB(SMXB)])
            vop((lambda SM, SV: lambda v: v.reduce_max(SV(0), SM(0), axis=AX.X, negate=True))(SM, SV),
                [SB(SMXB)], [SB(SMXB)])
            vop((lambda SM, SV: lambda sc: sc.activation(SM(1), SM(0), AF.Exp, bias=SV(0)))(SM, SV),
                [SB(SMXB)], [SB(SMXB)], engine="scalar")
            vop((lambda SM, SV: lambda v: v.reduce_sum(SV(1), SM(1), axis=AX.X))(SM, SV),
                [SB(SMXB)], [SB(SMXB)])
            vop((lambda SV: lambda v: v.reciprocal(SV(2), SV(1)))(SV),
                [SB(SMXB)], [SB(SMXB)])
            vop((lambda SM, SV: lambda v: v.tensor_scalar(SM(0), SM(1), SV(2), None, op0=ALU.mult))(SM, SV),
                [SB(SMXB)], [SB(SMXB)])
            vop((lambda SM: lambda v: v.tensor_copy(SM(2), SM(0)))(SM), [SB(SMXB)], [SB(SMXB)])
            for it in range(4):
                wsrc, wdst = (2, 3) if it % 2 == 0 else (3, 2)
                vop((lambda wsrc, SM, SV: lambda v: v.reduce_max(SV(3), SM(wsrc), axis=AX.X))(wsrc, SM, SV),
                    [SB(SMXB)], [SB(SMXB)])
                vop((lambda sub, it, SV: lambda v: v.tensor_copy(
                    F("wsave", (g * SG + sub) * 4 + it, 1), SV(3)))(sub, it, SV),
                    [SB(SMXB)], [SB("wsave")])
                vop((lambda wsrc, sub, it, SM, SV: lambda v: v.tensor_scalar(
                    F("eqsv", (sub * 4 + it) * 16, 16, w=E) if False else
                    F("eqsv", (sub * 4 + it) * 16, E),
                    SM(wsrc), SV(3), None, op0=ALU.is_ge))(wsrc, sub, it, SM, SV),
                    [SB(SMXB)], [SB("eqsv")])
                if it < 3:
                    vop((lambda wsrc, wdst, sub, it, SM: lambda v: v.scalar_tensor_tensor(
                        SM(wdst), F("eqsv", (sub * 4 + it) * 16, E), -2.0, SM(wsrc),
                        op0=ALU.mult, op1=ALU.add))(wsrc, wdst, sub, it, SM),
                        [SB(SMXB), SB("eqsv")], [SB(SMXB)])
            vop((lambda SM, SV: lambda v: v.tensor_scalar(SM(4), SM(0), SV(3), None, op0=ALU.is_ge))(SM, SV),
                [SB(SMXB)], [SB(SMXB)])
            vop((lambda SM: lambda v: v.tensor_tensor(SM(1), SM(0), SM(4), op=ALU.mult))(SM),
                [SB(SMXB)], [SB(SMXB)])
            vop((lambda SM, SV: lambda v: v.reduce_sum(SV(4), SM(1), axis=AX.X))(SM, SV),
                [SB(SMXB)], [SB(SMXB)])
            vop((lambda SV: lambda v: v.tensor_scalar(SV(5), SV(4), 1e-9, None, op0=ALU.max))(SV),
                [SB(SMXB)], [SB(SMXB)])
            vop((lambda SV: lambda v: v.reciprocal(SV(6), SV(5)))(SV),
                [SB(SMXB)], [SB(SMXB)])
            for k in range(4):
                vop((lambda sub, k, SV: lambda v: v.tensor_scalar(
                    F("wsave", (g * SG + sub) * 4 + k, 1),
                    F("wsave", (g * SG + sub) * 4 + k, 1), SV(6), None, op0=ALU.mult))(sub, k, SV),
                    [SB(SMXB)], [SB("wsave")])
            vop((lambda SM: lambda v: v.tensor_copy(B("keepbf", 0, E), SM(4)))(SM),
                [SB(SMXB)], [SB("keepbf")])
            mm(bank_buf[0], pbbf[0:E, 0:128],
               [SB("keepbf"), SB("identb2")],
               B("keepbf", 0, E), None, True, True, transpose_in=B("identb"))
            vop((lambda sub: lambda v: v.tensor_copy(
                F("r2", sub * 128, 128, p0=0, pn=E), pbbf[0:E, 0:128]))(sub),
                [bank_buf[0]], [SB("r2")])

        def MW(col, w, pn=16):
            o, _ = f32_ffn["r2"]
            return af32[0:pn, o + col:o + col + w]

        def emit_sparse_prep(g):
            for i in range(SG):
                init = 0.0 if i == 0 else MW(G + i * 128 - 1, 1)
                vop((lambda i, init: lambda v: v.tensor_tensor_scan(
                    MW(G + i * 128, 128),
                    F("onesf2", 0, 128, p0=0, pn=16),
                    MW(i * 128, 128),
                    init, op0=ALU.mult, op1=ALU.add))(i, init),
                    [SB("onesf2"), SB("r2"), SB("x2tmc")], [SB("r2")])
            vop(lambda v: v.tensor_tensor(
                MW(2 * G, G), MW(G, G), MW(0, G), op=ALU.subtract),
                [SB("r2")], [SB("r2"), SB("x2tmc")])
            for sub in range(SG):
                mm(bank_buf[7], bank(7, 0, 16),
                   [SB("r2"), SB("x2tmc"), SB("identf")],
                   MW(2 * G + sub * 128, 128),
                   None, True, True, transpose_in=F("identf", 0, 16, p0=0, pn=16))
                vop(lambda v: v.tensor_copy(F("rktm", 0, 16), bank(7, 0, 16)),
                    [bank_buf[7]], [SB("rktm")])
                sload(B("xtmrow", 0, D), d_x2bf[g * G + sub * 128:g * G + (sub + 1) * 128, :],
                      SB("xtmrow"), stream="xtr")
                for k in range(4):
                    eq = (lambda sub, k: F("eqsv", (sub * 4 + k) * 16, E))(sub, k)
                    vop((lambda sub, k: lambda v: v.tensor_tensor(
                        F("sprs", 0, 16, w=E) if False else F("sprs", 0, E),
                        F("eqsv", (sub * 4 + k) * 16, E), F("iota16", 0, E), op=ALU.mult))(sub, k),
                        [SB("eqsv"), SB("iota16")], [SB("sprs")])
                    vop(lambda v: v.reduce_sum(F("sprs", 16, 1), F("sprs", 0, E), axis=AX.X),
                        [SB("sprs")], [SB("sprs")])
                    vop((lambda sub, k: lambda v: v.tensor_tensor(
                        F("sprs", 0, E), F("eqsv", (sub * 4 + k) * 16, E), F("rktm", 0, E), op=ALU.mult))(sub, k),
                        [SB("eqsv"), SB("rktm")], [SB("sprs")])
                    vop(lambda v: v.reduce_sum(F("sprs", 17, 1), F("sprs", 0, E), axis=AX.X),
                        [SB("sprs")], [SB("sprs")])
                    vop(lambda v: v.scalar_tensor_tensor(
                        F("sprs", 18, 1), F("sprs", 16, 1), float(CC), F("sprs", 17, 1),
                        op0=ALU.mult, op1=ALU.add),
                        [SB("sprs")], [SB("sprs")])
                    vop((lambda g: lambda v: v.tensor_scalar(
                        F("sprs", 19, 1), F("sprs", 18, 1), float(g * C + 1e7), None, op0=ALU.add))(g),
                        [SB("sprs")], [SB("sprs")])
                    vop(lambda v: v.tensor_scalar(
                        F("sprs", 20, 1), F("sprs", 17, 1), float(C) - 0.5, None, op0=ALU.is_le),
                        [SB("sprs")], [SB("sprs")])
                    vop(lambda v: v.scalar_tensor_tensor(
                        F("sprs", 21, 1), F("sprs", 20, 1), -1e7, F("sprs", 19, 1),
                        op0=ALU.mult, op1=ALU.add),
                        [SB("sprs")], [SB("sprs")])
                    col = (g * SG + sub) * 4 + k
                    vop((lambda col: lambda v: v.tensor_copy(idxsb[:, col:col + 1], F("sprs", 21, 1)))(col),
                        [SB("sprs")], [SB("idx")])
                    tok = pop("gpsimd", (lambda col: lambda gp: gp.indirect_dma_start(
                        out=d_xc, out_offset=bass.IndirectOffsetOnAxis(idxsb[:, col:col + 1], 0),
                        in_=B("xtmrow", 0, D), in_offset=None,
                        bounds_check=bcreg(gp), oob_is_err=False))(col),
                        waits=SB("idx").rd_waits() + SB("xtmrow").rd_waits() + [SB("sct%d" % k).writers[-1] if SB("sct%d" % k).writers else None],
                        inc=gs("sct%d" % k), inc_n=16)
                    SB("idx").add_reader(tok)
                    SB("xtmrow").add_reader(tok)
                    SB("sct%d" % k).set_writer(tok)

        def emit_xc_transposes(e):
            def mk(fc):
                def fn(base):
                    return pop("sync", (lambda fc: lambda ee: ee.dma_start_transpose(
                        out=B("x2sb", fc * CC, CC),
                        in_=d_xc[e * CC:(e + 1) * CC, fc * 128:(fc + 1) * 128]))(fc),
                        waits=base + scat_end, inc=gs("sl_xct"), inc_n=16)
                return fn
            multi_writes(SB("x2sb"), [mk(fc) for fc in range(FC)])

        def emit_sparse_expert2(e):
            for ib in range(IB):
                p = ib % 2
                gload("wgb%d" % p, B("wgb%d" % p), d_wgate[e, ib], SB("wgb%d" % p))
                gload("wub%d" % p, B("wub%d" % p), d_wup[e, ib], SB("wub%d" % p))
                gul = {}
                for kind in range(2):
                    wname = ("wgb%d" if kind == 0 else "wub%d") % p
                    for h, (h0, hw) in enumerate(CHALF):
                        bk = 1 + (p * 2 * NCHALF + kind * NCHALF + h) % 7
                        gul[(kind, h)] = bk
                        for fc in range(FC):
                            mm(bank_buf[bk], bank(bk, 0, hw),
                               [SB(wname), SB("x2sb")],
                               B(wname, fc * 128, 128),
                               B("x2sb", fc * CC + h0, hw),
                               fc == 0, fc == FC - 1)
                        if kind == 0:
                            vop((lambda h, hw, bk: lambda sc: sc.activation(
                                F("siluscr", h * 512, hw), bank(bk, 0, hw), AF.Silu))(h, hw, bk),
                                [bank_buf[bk]], [SB("siluscr%d" % h)], engine="scalar")
                for h, (h0, hw) in enumerate(CHALF):
                    ubk = gul[(1, h)]
                    vop((lambda ib, h0, hw, ubk, h: lambda v: v.tensor_tensor(
                        B("tsb", ib * CC + h0, hw), F("siluscr", h * 512, hw),
                        bank(ubk, 0, hw), op=ALU.mult))(ib, h0, hw, ubk, h),
                        [SB("siluscr%d" % h), bank_buf[ubk]], [SB("tsb")])
            if e + 1 < E:
                emit_xc_transposes(e + 1)
            for fo in range(FOD):
                p = fo % 2
                gload("wdb%d" % p, B("wdb%d" % p), d_wdown[e, fo], SB("wdb%d" % p))
                for sub in range(CSUB):
                    bk = 1 + (fo * CSUB + sub) % 7
                    for ib in range(IB):
                        mm(bank_buf[bk], bank(bk),
                           [SB("tsb"), SB("wdb%d" % p)],
                           B("tsb", ib * CC + sub * 128, 128),
                           B("wdb%d" % p, ib * 512, 512),
                           ib == 0, ib == IB - 1)
                    ysl = (sub % 2) * 2 + fo % 2
                    vop((lambda bk, ysl: lambda v: v.tensor_copy(
                        B("ygat", ysl * 512, 512), bank(bk)))(bk, ysl),
                        [bank_buf[bk]], [SB("ygs%d" % ysl)])
                    store(d_yc[e * CC + sub * 128:e * CC + (sub + 1) * 128,
                               fo * 512:(fo + 1) * 512],
                          SB("ygs%d" % ysl), B("ygat", ysl * 512, 512),
                          stream="yc%d" % ysl)

        def seed_ygat():
            for yp in range(2):
                for s4 in range(4):
                    SB("ygat%d" % yp).readers += SB("ygs%d" % s4).readers
                    SB("ygat%d" % yp).writers += SB("ygs%d" % s4).writers

        def init_routed():
            base = SB("routed")
            ws = base.wr_waits()
            toks = []
            for s in range(SG):
                toks.append(pop("vector", (lambda s: lambda v: v.memset(F("routed", s * D, D), 0.0))(s),
                                waits=ws, inc=dve))
            base.writers = toks
            base.readers = []

        def emit_shared(g):
            for ee in range(E, E + SHQ):
                emit_dense_expert(g, ee)

        def emit_backward_sub(g, sub):
            if True:
                for k in range(4):
                    yp = k % 2
                    col = (g * SG + sub) * 4 + k
                    gtok = pop("gpsimd", (lambda col, yp: lambda gp: gp.indirect_dma_start(
                        out=B("ygat", yp * D, D), out_offset=None,
                        in_=d_yc, in_offset=bass.IndirectOffsetOnAxis(idxsb[:, col:col + 1], 0),
                        bounds_check=bcreg(gp), oob_is_err=False))(col, yp),
                        waits=SB("ygat%d" % yp).wr_waits() + SB("idx").rd_waits() + yc_end,
                        inc=gs("gyc%d" % yp), inc_n=16)
                    SB("ygat%d" % yp).set_writer(gtok)
                    SB("idx").add_reader(gtok)
                    vop((lambda sub, k, yp, col: lambda v: v.scalar_tensor_tensor(
                        F("routed", sub * D, D), B("ygat", yp * D, D),
                        F("wsave", col, 1), F("routed", sub * D, D),
                        op0=ALU.mult, op1=ALU.add))(sub, k, yp, col),
                        [SB("ygat%d" % yp), SB("wsave"), SB("routed")], [SB("routed")])

        def emit_combine_sub(g, sub):
            if True:
                ch = g * SG + sub
                sload(F("x2tmc"), d_x2tm[ch * 128:(ch + 1) * 128, :], SB("x2tmc"),
                      extra=attn_end, stream="x2tmc")
                vop((lambda sub: lambda v: v.tensor_tensor(
                    F("r2", 0, D), F("x2tmc", 0, D), F("routed", sub * D, D), op=ALU.add))(sub),
                    [SB("x2tmc"), SB("routed")], [SB("r2")])
                emit_rowln(SB("r2"), lambda lo, w: F("r2", lo, w),
                           SB("x2tmc"), lambda lo, w: F("x2tmc", lo, w),
                           D, SB("out2"), lambda lo, w: F("out2", lo, w), "lnv2", epsn="eps2")
                store(d_out[ch * 128:(ch + 1) * 128, :], SB("out2"), F("out2", 0, D), stream="out")

        def emit_dense_expert(g, ee):
            qq = ee - E
            wg_src = lambda ib: d_shg[qq * IB + ib]
            wu_src = lambda ib: d_shu[qq * IB + ib]
            wd_src = lambda fo: d_shd[qq, fo]
            col = lambda sub: (g * SG + sub) * EW + E
            for ib in range(IB):
                p = ib % 2
                gload("wgb%d" % p, B("wgb%d" % p), wg_src(ib), SB("wgb%d" % p))
                gload("wub%d" % p, B("wub%d" % p), wu_src(ib), SB("wub%d" % p))
                gul = {}
                for kind in range(2):
                    wname = ("wgb%d" if kind == 0 else "wub%d") % p
                    for h, (h0, hw) in enumerate(GHALF):
                        bk = 1 + (p * 2 * NHALF + kind * NHALF + h) % 7
                        gul[(kind, h)] = bk
                        for fc in range(FC):
                            mm(bank_buf[bk], bank(bk, 0, hw),
                               [SB(wname), SB("x2sb")],
                               B(wname, fc * 128, 128),
                               B("x2sb", fc * G + h0, hw),
                               fc == 0, fc == FC - 1)
                        if kind == 0:
                            vop((lambda h, hw, bk: lambda sc: sc.activation(
                                F("siluscr", h * 512, hw), bank(bk, 0, hw), AF.Silu))(h, hw, bk),
                                [bank_buf[bk]], [SB("siluscr%d" % h)], engine="scalar")
                for h, (h0, hw) in enumerate(GHALF):
                    ubk = gul[(1, h)]
                    vop((lambda ib, h0, hw, ubk, h: lambda v: v.tensor_tensor(
                        B("tsb", ib * G + h0, hw), F("siluscr", h * 512, hw),
                        bank(ubk, 0, hw), op=ALU.mult))(ib, h0, hw, ubk, h),
                        [SB("siluscr%d" % h), bank_buf[ubk]], [SB("tsb")])
            for fo in range(FOD):
                p = fo % 2
                gload("wdb%d" % p, B("wdb%d" % p), wd_src(fo), SB("wdb%d" % p))
                for sub in range(SG):
                    bk = 1 + (fo * SG + sub) % 7
                    for ib in range(IB):
                        mm(bank_buf[bk], bank(bk),
                           [SB("tsb"), SB("wdb%d" % p)],
                           B("tsb", ib * G + sub * 128, 128),
                           B("wdb%d" % p, ib * 512, 512),
                           ib == 0, ib == IB - 1)
                    vop((lambda sub, fo, bk, cc: lambda v: v.scalar_tensor_tensor(
                        F("routed", sub * D + fo * 512, 512), bank(bk),
                        F("comb", cc, 1),
                        F("routed", sub * D + fo * 512, 512),
                        op0=ALU.mult, op1=ALU.add))(sub, fo, bk, col(sub)),
                        [bank_buf[bk], SB("comb"), SB("routed")], [SB("routed")])

        def emit_bc_interleaved(g):
            for sub in range(SG):
                emit_backward_sub(g, sub)
                emit_combine_sub(g, sub)

        def emit_router_group(g):
            load_x2sb(g)
            for sub in range(SG):
                bk = [5, 6][sub % 2]
                for fc in range(FC):
                    mm(bank_buf[bk], bank(bk, 0, EW),
                       [SB("rwtsb_b"), SB("x2sb")],
                       B("x2sb", fc * G + sub * 128, 128),
                       B("rwtsb", fc * EW, EW),
                       fc == 0, fc == FC - 1)
                emit_router_sub(g, sub, bk)

        emit_router_group(0)
        emit_sparse_prep(0)
        emit_router_group(1)
        emit_sparse_prep(1)
        init_routed()
        emit_shared(1)
        scat_end = [Tok(gs("sct%d" % k), gs("sct%d" % k).count) for k in range(4)]
        emit_xc_transposes(0)
        for e in range(E):
            emit_sparse_expert2(e)
        yc_end = [Tok(gs("st_yc%d" % i), gs("st_yc%d" % i).count) for i in range(4)]
        seed_ygat()
        emit_bc_interleaved(1)
        load_x2sb(0)
        init_routed()
        emit_shared(0)
        emit_bc_interleaved(0)

        final_waits = [Tok(s, s.count) for n, s in gsem.items() if n.startswith("st_")]
        P.op("sync", lambda e: e.nop(), waits=final_waits)
        P.emit(nc, es)
    return nc



def stage_core(x_main, x_prefix, cfg):
    M, D, FC, SC, NSC, NSCP = cfg["M"], cfg["D"], cfg["FC"], cfg["SC"], cfg["NSC"], cfg["NSCP"]
    def fm(x, nsc):
        xt = np.ascontiguousarray(x.T)
        return np.ascontiguousarray(
            xt.reshape(FC, 128, nsc, SC).transpose(2, 1, 0, 3)).reshape(nsc, 128, FC * SC)
    return {
        "xfm": fm(x_main, NSC),
        "xpfm": fm(x_prefix, max(NSCP, 1)),
        "xtm": np.ascontiguousarray(x_main.reshape(cfg["NCH"], 128, D)),
    }


def stage_weights(w, cfg):
    D, FC, H, E, IB, SHQ, FOD = (cfg["D"], cfg["FC"], cfg["H"], cfg["E"], cfg["IB"],
                                 cfg["SHQ"], cfg["FOD"])
    st = {}
    def occhunks(W, nout):
        return np.ascontiguousarray(
            W.reshape(FC, 128, nout, 128).transpose(2, 1, 0, 3)).reshape(nout, 128, FC * 128)
    def quarters(W):
        return np.ascontiguousarray(
            W.reshape(FC // 4, 4, 128, FOD, 512).transpose(3, 0, 2, 1, 4)).reshape(
                FOD, FC // 4, 128, 4 * 512)
    st["wq"] = occhunks(w["Wq"], H)
    st["wk"] = occhunks(w["Wk"], H)
    st["wkg1"] = np.ascontiguousarray(
        w["Wkg1"].reshape(FC, 128, 16).transpose(1, 0, 2)).reshape(128, FC * 16)
    st["wkg2"] = np.ascontiguousarray(w["Wkg2"])
    st["bkg2s"] = np.ascontiguousarray(w["bkg2"].reshape(H, 128).T)
    st["wv"] = quarters(w["Wv"])
    st["wg"] = occhunks(w["Wg"], FC)
    st["bgs"] = np.ascontiguousarray(w["bg"].reshape(FC, 128).T)
    st["wo"] = quarters(w["Wo"])
    rw_sg = np.concatenate([w["router_w"].T, w["sh_gate_w"]], axis=1)
    ew = E + 1
    st["rwt"] = np.ascontiguousarray(
        rw_sg.reshape(FC, 128, ew).transpose(1, 0, 2)).reshape(128, FC * ew)
    st["wgate"] = np.ascontiguousarray(
        w["w_gate"].reshape(E, FC, 128, IB, 128).transpose(0, 3, 2, 1, 4)).reshape(
            E, IB, 128, FC * 128)
    st["wup"] = np.ascontiguousarray(
        w["w_up"].reshape(E, FC, 128, IB, 128).transpose(0, 3, 2, 1, 4)).reshape(
            E, IB, 128, FC * 128)
    st["wdown"] = np.ascontiguousarray(
        w["w_down"].reshape(E, IB, 128, FOD, 512).transpose(0, 3, 2, 1, 4)).reshape(
            E, FOD, 128, IB * 512)
    st["shg"] = np.ascontiguousarray(
        w["sh_gate"].reshape(FC, 128, SHQ * IB, 128).transpose(2, 1, 0, 3)).reshape(
            SHQ * IB, 128, FC * 128)
    st["shu"] = np.ascontiguousarray(
        w["sh_up"].reshape(FC, 128, SHQ * IB, 128).transpose(2, 1, 0, 3)).reshape(
            SHQ * IB, 128, FC * 128)
    st["shd"] = np.ascontiguousarray(
        w["sh_down"].reshape(SHQ, IB, 128, FOD, 512).transpose(0, 3, 2, 1, 4)).reshape(
            SHQ, FOD, 128, IB * 512)
    st["iota16"] = np.ascontiguousarray(np.tile(np.arange(16, dtype=np.float32), (128, 1)))
    st["identf"] = np.eye(128, dtype=np.float32)
    st["masku4"] = np.ascontiguousarray(
        np.tile(np.triu(np.ones((128, 128), np.float32)), (1, 4)))
    st["identbf"] = np.eye(128).astype(BFD)
    return st


def kernel(**inputs):
    cfg = make_cfg(mini=False)
    x = np.asarray(inputs["L0_attn_x"])
    w = {
        "Wq": inputs["L0_attn_Wq"], "Wk": inputs["L0_attn_Wk"],
        "Wkg1": inputs["L0_attn_Wkg1"], "Wkg2": inputs["L0_attn_Wkg2"],
        "bkg2": inputs["L0_attn_bkg2"], "Wv": inputs["L0_attn_Wv"],
        "Wg": inputs["L0_attn_Wg"], "bg": inputs["L0_attn_bg"],
        "Wo": inputs["L0_attn_Wo"], "router_w": inputs["L0_ffn_router_w"],
        "w_gate": inputs["L0_ffn_w_gate"], "w_up": inputs["L0_ffn_w_up"],
        "w_down": inputs["L0_ffn_w_down"], "sh_gate": inputs["L0_ffn_sh_gate"],
        "sh_up": inputs["L0_ffn_sh_up"], "sh_down": inputs["L0_ffn_sh_down"],
        "sh_gate_w": inputs["L0_ffn_sh_gate_w"],
    }
    w = {k: np.asarray(v, np.float32) for k, v in w.items()}
    wst = stage_weights(w, cfg)
    nc = build_program(cfg)
    in_maps = []
    zeros_pfx = np.zeros((cfg["PFX"], cfg["D"]), np.float32)
    for core in range(8):
        b, half = core // 2, core % 2
        xm = np.ascontiguousarray(x[b, half * 2048:(half + 1) * 2048, :], dtype=np.float32)
        xp = (np.ascontiguousarray(x[b, 0:2048, :], dtype=np.float32)
              if half == 1 else zeros_pfx)
        m = dict(wst)
        m.update(stage_core(xm, xp, cfg))
        in_maps.append(m)
    res = run_bass_kernel_spmd(nc, in_maps, core_ids=list(range(8)))
    out = np.empty((4, 4096, 2048), np.float32)
    for core in range(8):
        b, half = core // 2, core % 2
        out[b, half * 2048:(half + 1) * 2048, :] = res.results[core]["out"]
    return out


# version 08 (final)
# speedup vs baseline: 1.2509x; 1.0254x over previous
import numpy as np
import ml_dtypes
import concourse.bass as bass
import concourse.mybir as mybir
from concourse.bass_utils import run_bass_kernel_spmd

F32 = mybir.dt.float32
BF16 = mybir.dt.bfloat16
AF = mybir.ActivationFunctionType
ALU = mybir.AluOpType
AX = mybir.AxisListType
BFD = ml_dtypes.bfloat16


class Sem:
    def __init__(self, name):
        self.name = name
        self.count = 0
        self.h = None

class Tok:
    __slots__ = ("sem", "val")
    def __init__(self, sem, val):
        self.sem = sem
        self.val = val

class Plan:
    def __init__(self):
        self.ops = []
        self.sems = {}

    def sem(self, name):
        if name not in self.sems:
            self.sems[name] = Sem(name)
        return self.sems[name]

    def op(self, engine, emit_fn, waits=(), inc=None, inc_n=1):
        waits = [w for w in waits if w is not None]
        tok = None
        incs = []
        if inc is not None:
            inc.count += inc_n
            tok = Tok(inc, inc.count)
            incs.append((inc, inc_n))
        self.ops.append((engine, emit_fn, waits, incs))
        return tok

    def boundary(self):
        return [Tok(s, s.count) for s in self.sems.values()]

    def emit(self, nc, es):
        for s in self.sems.values():
            s.h = es.enter_context(nc.semaphore(s.name))
        blk = es.enter_context(nc.Block())
        by_eng = {}
        for rec in self.ops:
            by_eng.setdefault(rec[0], []).append(rec)

        def run_engine(eng_obj, recs):
            watermark = {}
            for _, emit_fn, waits, incs in recs:
                by_sem = {}
                for w in waits:
                    if by_sem.get(w.sem.name, (None, -1))[1] < w.val:
                        by_sem[w.sem.name] = (w.sem, w.val)
                for name, (sem, val) in by_sem.items():
                    if watermark.get(name, -1) >= val:
                        continue
                    eng_obj.wait_ge(sem.h, val)
                    watermark[name] = val
                instr = emit_fn(eng_obj)
                for s, n in incs:
                    instr.then_inc(s.h, n)

        for name in ("sync", "scalar", "vector", "tensor", "gpsimd"):
            if name not in by_eng:
                continue
            recs = by_eng[name]
            section = getattr(blk, name)
            def make(rs):
                def _f(eng_obj):
                    run_engine(eng_obj, rs)
                return _f
            section(make(recs))
        return blk

class Buf:
    __slots__ = ("writers", "readers")
    def __init__(self):
        self.writers = []
        self.readers = []

    def rd_waits(self):
        return list(self.writers)

    def add_reader(self, tok):
        if tok is not None:
            self.readers.append(tok)

    def wr_waits(self):
        return self.readers + self.writers

    def set_writer(self, tok):
        self.writers = [tok] if tok is not None else []
        self.readers = []

    def add_writer(self, tok):
        if tok is not None:
            self.writers.append(tok)


def make_cfg(mini=False):
    c = {}
    if mini == "midi":
        c.update(D=2048, E=16, IB=2, SHQ=2, M=2048, PFX=512, ROWW=512)
    elif mini:
        c.update(D=512, E=4, IB=2, SHQ=2, M=1024, PFX=512, ROWW=128)
    else:
        c.update(D=2048, E=16, IB=11, SHQ=4, M=2048, PFX=2048, ROWW=512)
    D = c["D"]
    c["FC"] = D // 128
    c["H"] = D // 256
    c["SC"] = 512
    c["CPS"] = 4
    c["NSC"] = c["M"] // c["SC"]
    c["NSCP"] = c["PFX"] // c["SC"]
    c["G"] = c["M"] // 2
    c["SG"] = c["G"] // 128
    c["FOD"] = D // 512
    c["GHALF"] = [(h * 512, min(512, c["G"] - h * 512)) for h in range((c["G"] + 511) // 512)]
    c["NCH"] = c["M"] // 128
    c["C"] = min(384, c["G"])
    if c["E"] <= 4:
        c["C"] = c["G"]
    return c


def build_program(cfg, debug_spill=False):
    D, FC, H, E, IB, SHQ = cfg["D"], cfg["FC"], cfg["H"], cfg["E"], cfg["IB"], cfg["SHQ"]
    M, PFX, SC, G, SG, FOD = cfg["M"], cfg["PFX"], cfg["SC"], cfg["G"], cfg["SG"], cfg["FOD"]
    NSC, NSCP, ROWW, CPS = cfg["NSC"], cfg["NSCP"], cfg["ROWW"], cfg["CPS"]
    GHALF, NCH = cfg["GHALF"], cfg["NCH"]
    NHALF = len(GHALF)
    C = cfg["C"]
    CC = 2 * C
    CHALF = [(h * 512, min(512, CC - h * 512)) for h in range((CC + 511) // 512)]
    NCHALF = len(CHALF)
    CSUB = CC // 128
    EW = E + 1
    HP = H * 128
    NBO = (2 * HP) // 512 if 2 * HP >= 512 else 1
    NAB = (H + 3) // 4
    NP = (HP + ROWW - 1) // ROWW
    SCALING = 0.0625
    S16 = 1.0 / 16.0

    nc = bass.Bass("TRN2", target_bir_lowering=False)

    def din(name, shape, dt=F32):
        return nc.dram_tensor(name, shape, dt, kind="ExternalInput").ap()

    d_xfm = din("xfm", [NSC, 128, FC * SC])
    d_xpfm = din("xpfm", [max(NSCP, 1), 128, FC * SC])
    d_xtm = din("xtm", [NCH, 128, D])
    d_wq = din("wq", [H, 128, FC * 128])
    d_wk = din("wk", [H, 128, FC * 128])
    d_wkg1 = din("wkg1", [128, FC * 16])
    d_wkg2 = din("wkg2", [16, H * 128])
    d_bkg2 = din("bkg2s", [128, H])
    d_wv = din("wv", [FOD, FC // 4, 128, 4 * 512])
    d_wg = din("wg", [FC, 128, FC * 128])
    d_bg = din("bgs", [128, FC])
    d_wo = din("wo", [FOD, FC // 4, 128, 4 * 512])
    d_rwt = din("rwt", [128, FC * EW])
    d_wgate = din("wgate", [E, IB, 128, FC * 128])
    d_wup = din("wup", [E, IB, 128, FC * 128])
    d_wdown = din("wdown", [E, FOD, 128, IB * 512])
    d_shg = din("shg", [SHQ * IB, 128, FC * 128])
    d_shu = din("shu", [SHQ * IB, 128, FC * 128])
    d_shd = din("shd", [SHQ, FOD, 128, IB * 512])
    d_iota16 = din("iota16", [128, 16])
    d_identf = din("identf", [128, 128])
    d_masku = din("masku4", [128, 4 * 128])
    d_ident = din("identbf", [128, 128], BF16)

    d_out = nc.dram_tensor("out", [M, D], F32, kind="ExternalOutput").ap()
    spill_kind = "ExternalOutput" if debug_spill else "Internal"
    d_x2tm = nc.dram_tensor("x2tm_sp", [M, D], F32, kind=spill_kind).ap()
    d_x2bf = nc.dram_tensor("x2bf_sp", [M, D], BF16, kind=spill_kind).ap()
    d_xc = nc.dram_tensor("xc_sp", [E * CC, D], BF16, kind=spill_kind).ap()
    d_yc = nc.dram_tensor("yc_sp", [E * CC, D], BF16, kind=spill_kind).ap()

    from contextlib import ExitStack
    es = ExitStack()
    with es:
        def mklay(pairs):
            lay, off = {}, 0
            for name, w in pairs:
                w = (w + 15) // 16 * 16
                lay[name] = (off, w)
                off += w
            return lay, off

        bf_attn, n1 = mklay([
            ("xsb", FC * SC), ("qd", H * SC), ("kneg", H * SC), ("kdfm", H * SC),
            ("kdtm", H * 128), ("vtm0", D), ("vtm1", D),
            ("silug", FC * 256), ("og0", FC * 128), ("og1", FC * 128),
            ("Asb", H * 128), ("Sbf", H * 256), ("kg1", 512),
            ("wp0", max(FC * 128, 2048)), ("wp1", max(FC * 128, 2048)), ("wp2", max(FC * 128, 2048)), ("wp3", max(FC * 128, 2048)), ("wkg2sb", H * 128),
            ("obf", H * 256), ("sqbf", H * 256), ("rowbf", 2 * HP),
            ("x2bfc", D), ("identb", 128), ("wkg1sb", FC * 16),
        ])
        bf_ffn, n2 = mklay([
            ("x2sb", max(FC * G, FC * CC)), ("tsb", max(IB * G, IB * CC)),
            ("xtmrow", D), ("keepbf", 16), ("ygat", 2 * D), ("identb", 128),
            ("wgb0", FC * 128), ("wgb1", FC * 128),
            ("wub0", FC * 128), ("wub1", FC * 128),
            ("wdb0", IB * 512), ("wdb1", IB * 512),
            ("rwtsb", FC * EW),
        ])
        NBF = max(n1, n2) + 16

        f32_attn, m1 = mklay([
            ("zc", H * 128), ("scr1", H * 128), ("scr2", H * 128),
            ("gkc", max(H * SC, 3 * HP)),
            ("ehs", 2 * SC), ("dl", 64), ("gkcl", 64),
            ("xtmc0", D), ("xtmc1", D), ("rc0", D), ("rc1", D),
            ("x2c", D), ("rows", 2 * HP), ("lnv", 64),
            ("S", H * 256), ("masku", 4 * 128), ("onesf", 128),
            ("bkg2", 16), ("bgc", FC), ("eps", 16),
        ])
        f32_ffn, m2 = mklay([
            ("routed", SG * D), ("siluscr", NHALF * 512), ("smx", 2 * (6 * 16 + 64)),
            ("comb", 2 * SG * EW), ("r2", D), ("x2tmc", D), ("out2", D), ("lnv2", 64),
            ("eps2", 16), ("eqsv", SG * 4 * 16), ("wsave", 2 * SG * 4),
            ("rktm", 32), ("iota16", 16), ("sprs", 32), ("identf", 128), ("onesf2", 128),
        ])
        NF32 = max(m1, m2) + 16

        abf = es.enter_context(nc.sbuf_tensor("abf", [128, NBF], BF16))
        af32 = es.enter_context(nc.sbuf_tensor("af32", [128, NF32], F32))
        onesbc = es.enter_context(nc.sbuf_tensor("onesbc", [128, 1], BF16))
        U32 = mybir.dt.uint32
        idxsb = es.enter_context(nc.sbuf_tensor("idxsb", [128, 2 * SG * 4], U32))
        onesbr = es.enter_context(nc.sbuf_tensor("onesbr", [1, 128], BF16))

        PHASE = {"bf": bf_attn, "f32": f32_attn}

        def B(name, lo=0, w=None, p0=0, pn=128):
            o, ww = PHASE["bf"][name]
            if w is None:
                w = ww - lo
            assert lo + w <= ww, (name, lo, w, ww)
            return abf[p0:p0 + pn, o + lo: o + lo + w]

        def F(name, lo=0, w=None, p0=0, pn=128):
            o, ww = PHASE["f32"][name]
            if w is None:
                w = ww - lo
            assert lo + w <= ww, (name, lo, w, ww)
            return af32[p0:p0 + pn, o + lo: o + lo + w]

        def Fap(name, lo, dims):
            o, _ = PHASE["f32"][name]
            return bass.AP(af32, o + lo, [[NF32, 128]] + dims)

        def Bap(name, lo, dims):
            o, _ = PHASE["bf"][name]
            return bass.AP(abf, o + lo, [[NBF, 128]] + dims)

        pbbf = es.enter_context(nc.psum_tensor("pbbf", [128, 1024], BF16))
        pball = es.enter_context(nc.psum_tensor("pball", [128, 3584], F32))

        def bank(i, lo=0, w=512, p0=0, pn=128):
            assert 1 <= i <= 7 and lo + w <= 512
            return pball[p0:p0 + pn, (i - 1) * 512 + lo:(i - 1) * 512 + lo + w]

        P = Plan()
        pes = P.sem("pe")
        dve = P.sem("dve")
        act = P.sem("act")
        pool = P.sem("pool")
        sts = P.sem("st")
        lds = P.sem("lds")
        gsem = {}
        def gs(name):
            if name not in gsem:
                gsem[name] = P.sem("g_" + name)
            return gsem[name]

        bank_buf = {i: Buf() for i in range(8)}
        bufs = {}
        def SB(name):
            if name not in bufs:
                bufs[name] = Buf()
            return bufs[name]

        def pop(engine, emit_fn, waits=(), inc=None, inc_n=1):
            b_, f_ = PHASE["bf"], PHASE["f32"]
            def wrapped(e, emit_fn=emit_fn, b_=b_, f_=f_):
                PHASE["bf"], PHASE["f32"] = b_, f_
                return emit_fn(e)
            return P.op(engine, wrapped, waits=waits, inc=inc, inc_n=inc_n)

        def gload(stream, dst_ap, src_ap, buf, multi=False):
            tok = pop("gpsimd", (lambda d, s: lambda g: g.dma_start(out=d, in_=s))(dst_ap, src_ap),
                       waits=buf.wr_waits(), inc=gs(stream), inc_n=16)
            if multi:
                buf.add_writer(tok)
            else:
                buf.set_writer(tok)
            return tok

        def sload(dst_ap, src_ap, buf, transpose=False, extra=(), multi=False, stream="s"):
            if transpose:
                fn = (lambda d, s: lambda e: e.dma_start_transpose(out=d, in_=s))(dst_ap, src_ap)
            else:
                fn = (lambda d, s: lambda e: e.dma_start(out=d, in_=s))(dst_ap, src_ap)
            tok = pop("sync", fn, waits=list(extra) + (buf.writers if multi else buf.wr_waits()),
                       inc=gs("sl_" + stream), inc_n=16)
            if multi:
                buf.add_writer(tok)
            else:
                buf.set_writer(tok)
            return tok

        def store(dst_ap, src_buf, src_ap, stream="o"):
            tok = pop("sync", (lambda d, s: lambda e: e.dma_start(out=d, in_=s))(dst_ap, src_ap),
                       waits=src_buf.rd_waits(), inc=gs("st_" + stream), inc_n=16)
            src_buf.add_reader(tok)
            return tok

        open_grp = {}

        def mm(bk_buf, out_ap, in_bufs, lhsT, rhs, first, last, transpose_in=None,
               force_tok=False):
            waits = []
            if first:
                waits += bk_buf.wr_waits()
            for b in in_bufs:
                waits += b.rd_waits()
            if transpose_in is not None:
                fn = (lambda o, l, i: lambda t: t.transpose(o, l, i))(out_ap, lhsT, transpose_in)
            else:
                fn = (lambda o, l, r, f, s: lambda t: t.matmul(o, l, r, start=f, stop=s))(
                    out_ap, lhsT, rhs, first, last)
            tok = pop("tensor", fn, waits=waits, inc=(pes if (last or force_tok) else None))
            key = id(bk_buf)
            if first:
                open_grp[key] = []
            if tok is not None:
                for b in in_bufs:
                    b.add_reader(tok)
            else:
                open_grp.setdefault(key, []).extend(in_bufs)
            if last:
                grp = open_grp.pop(key, [])
                seen = set()
                for b in grp:
                    if id(b) not in seen:
                        seen.add(id(b))
                        b.add_reader(tok)
                bk_buf.set_writer(tok)
            return tok

        def vop(fn, reads, writes, waits=(), engine="vector", multi_write=False):
            sem = act if engine == "scalar" else (pool if engine == "gpsimd" else dve)
            ws = list(waits)
            for b in reads:
                ws += b.rd_waits()
            for b in writes:
                ws += (b.writers if multi_write else b.wr_waits())
            tok = pop(engine, fn, waits=ws, inc=sem)
            for b in reads:
                b.add_reader(tok)
            for b in writes:
                if multi_write:
                    b.add_writer(tok)
                else:
                    b.set_writer(tok)
            return tok

        vop(lambda v: v.memset(onesbc[:], 1.0), [], [SB("onesbc")])
        vop(lambda v: v.memset(onesbr[:], 1.0), [], [SB("onesbr")])
        vop(lambda v: v.memset(F("onesf"), 1.0), [], [SB("onesf")])
        vop(lambda v: v.memset(F("S"), 0.0), [], [SB("S")])
        sload(F("masku"), d_masku, SB("masku"), stream="masku")
        sload(F("bkg2", 0, H), d_bkg2, SB("bkg2"), stream="bkg2")
        sload(F("bgc", 0, FC), d_bg, SB("bgc"), stream="bgc")
        vop(lambda v: v.memset(F("eps", 0, 1), 1e-5), [], [SB("eps")])
        gload("ident", B("identb"), d_ident, SB("identb"))
        vop(lambda v: v.memset(B("x2bfc", 0, D), 0.0), [], [SB("x2bfc")])
        for zi in range((E * CC + 127) // 128):
            r0 = zi * 128
            rn_ = min(128, E * CC - r0)
            tok = pop("scalar", (lambda r0, rn_: lambda e: e.dma_start(
                out=d_xc[r0:r0 + rn_, :], in_=B("x2bfc", 0, D, pn=rn_)))(r0, rn_),
                waits=SB("x2bfc").rd_waits(), inc=gs("st_zxc"), inc_n=16)
            SB("x2bfc").add_reader(tok)
        gload("wkg1", B("wkg1sb", 0, FC * 16), d_wkg1, SB("wkg1"))
        gload("wkg2", B("wkg2sb", p0=0, pn=16), d_wkg2, SB("wkg2"))

        wp_slot = ["wp0", "wp1", "wp2", "wp3"]

        NQ = FC // 4

        def emit_attn_superchunk(sci, main):
            x_src = d_xfm[sci] if main else d_xpfm[sci]
            for xq in range(NQ):
                gload("xsb%d" % xq, B("xsb", xq * 4 * SC, 4 * SC),
                      x_src[:, xq * 4 * SC:(xq + 1) * 4 * SC], SB("xsb%d" % xq))

            for fc in range(FC):
                mm(bank_buf[7], bank(7, p0=0, pn=16, w=SC),
                   [SB("wkg1"), SB("xsb%d" % (fc // 4))],
                   B("wkg1sb", fc * 16, 16),
                   B("xsb", fc * SC, SC),
                   fc == 0, fc == FC - 1)
            vop(lambda v: v.tensor_copy(B("kg1", 0, SC, p0=0, pn=16), bank(7, p0=0, pn=16, w=SC)),
                [bank_buf[7]], [SB("kg1")])

            for c in range(CPS):
                bks = [1, 2] if c % 2 == 0 else [3, 4]
                for h in range(H):
                    bk = bks[h // 4]
                    q4 = h % 4
                    nh = min(4, H - (h // 4) * 4)
                    mm(bank_buf[bk], bank(bk, q4 * 128, 128),
                       [SB("wkg2"), SB("kg1")],
                       B("wkg2sb", h * 128, 128, p0=0, pn=16),
                       B("kg1", c * 128, 128, p0=0, pn=16),
                       q4 == 0, q4 == nh - 1)
                for h in range(H):
                    bk = bks[h // 4]
                    vop((lambda h, bk: lambda v: v.tensor_scalar(
                        F("zc", h * 128, 128), bank(bk, (h % 4) * 128, 128),
                        F("bkg2", h, 1), None, op0=ALU.add))(h, bk),
                        [bank_buf[bk], SB("bkg2")], [SB("zc")])
                vop(lambda sc: sc.activation(F("scr1", 0, HP), F("zc", 0, HP), AF.Abs),
                    [SB("zc")], [SB("scr1")], engine="scalar")
                vop(lambda sc: sc.activation(F("scr2", 0, HP), F("scr1", 0, HP), AF.Exp, scale=-1.0),
                    [SB("scr1")], [SB("scr2")], engine="scalar")
                vop(lambda sc: sc.activation(F("scr1", 0, HP), F("scr2", 0, HP), AF.Ln, bias=1.0),
                    [SB("scr2")], [SB("scr1")], engine="scalar")
                vop(lambda v: v.scalar_tensor_tensor(
                    F("scr2", 0, HP), F("zc", 0, HP), 0.0, F("scr1", 0, HP),
                    op0=ALU.min, op1=ALU.subtract),
                    [SB("zc"), SB("scr1")], [SB("scr2")])
                for h in range(H):
                    vop((lambda h, c: lambda v: v.tensor_tensor_scan(
                        F("gkc", h * SC + c * 128, 128), F("onesf"),
                        F("scr2", h * 128, 128), 0.0, op0=ALU.mult, op1=ALU.add))(h, c),
                        [SB("onesf"), SB("scr2")], [SB("gkc")])

            last_ap = Fap("gkc", 127, [[SC, H], [128, CPS]])
            vop(lambda sc: sc.activation(F("dl", 0, H * CPS), last_ap, AF.Exp, scale=S16),
                [SB("gkc")], [SB("dl")], engine="scalar")
            vop(lambda v: v.tensor_scalar(F("gkcl", 0, H * CPS), last_ap, S16, None, op0=ALU.mult),
                [SB("gkc")], [SB("gkcl")])

            if main:
                for oc in range(H):
                    p = oc % 4
                    slot = wp_slot[p]
                    gload("wp%d" % p, B(slot, 0, FC * 128), d_wq[oc], SB(slot))
                    bk = [5, 6, 7][oc % 3]
                    for fc in range(FC):
                        mm(bank_buf[bk], bank(bk),
                           [SB(slot), SB("xsb%d" % (fc // 4))],
                           B(slot, fc * 128, 128),
                           B("xsb", fc * SC, SC),
                           fc == 0, fc == FC - 1)
                    ep = oc % 2
                    vop((lambda oc, ep: lambda sc: sc.activation(
                        F("ehs", ep * SC, SC), F("gkc", oc * SC, SC), AF.Exp, scale=S16))(oc, ep),
                        [SB("gkc")], [SB("ehs%d" % ep)], engine="scalar")
                    vop((lambda oc, bk, ep: lambda v: v.tensor_tensor(
                        B("qd", oc * SC, SC), F("ehs", ep * SC, SC), bank(bk), op=ALU.mult))(oc, bk, ep),
                        [SB("ehs%d" % ep), bank_buf[bk]], [SB("qd")])

            for oc in range(H):
                p = oc % 4
                slot = wp_slot[p]
                gload("wp%d" % p, B(slot, 0, FC * 128), d_wk[oc], SB(slot))
                bk = [5, 6, 7][oc % 3] if not main else [1, 2, 3][oc % 3]
                for fc in range(FC):
                    mm(bank_buf[bk], bank(bk),
                       [SB(slot), SB("xsb%d" % (fc // 4))],
                       B(slot, fc * 128, 128),
                       B("xsb", fc * SC, SC),
                       fc == 0, fc == FC - 1)
                ep = oc % 2
                if main:
                    vop((lambda oc, ep: lambda sc: sc.activation(
                        F("ehs", ep * SC, SC), F("gkc", oc * SC, SC), AF.Exp, scale=-S16))(oc, ep),
                        [SB("gkc")], [SB("ehs%d" % ep)], engine="scalar")
                    vop((lambda oc, bk, ep: lambda v: v.scalar_tensor_tensor(
                        B("kneg", oc * SC, SC), bank(bk), SCALING, F("ehs", ep * SC, SC),
                        op0=ALU.mult, op1=ALU.mult))(oc, bk, ep),
                        [SB("ehs%d" % ep), bank_buf[bk]], [SB("kneg")])
                for c in range(CPS):
                    vop((lambda oc, c, ep: lambda sc: sc.activation(
                        F("ehs", ep * SC + c * 128, 128), F("gkc", oc * SC + c * 128, 128), AF.Exp,
                        scale=-S16, bias=F("gkcl", oc * CPS + c, 1)))(oc, c, ep),
                        [SB("gkc"), SB("gkcl")], [SB("ehs%d" % ep)], engine="scalar")
                vop((lambda oc, bk, ep: lambda v: v.scalar_tensor_tensor(
                    B("kdfm", oc * SC, SC), bank(bk), SCALING, F("ehs", ep * SC, SC),
                    op0=ALU.mult, op1=ALU.mult))(oc, bk, ep),
                    [SB("ehs%d" % ep), bank_buf[bk]], [SB("kdfm")])

            def emit_g(hc):
                c0 = 2 * hc
                if main:
                    for oc in range(FC):
                        p = oc % 4
                        slot = wp_slot[p]
                        gload("wp%d" % p, B(slot, 0, FC * 128), d_wg[oc], SB(slot))
                        bk = [5, 6, 7][oc % 3]
                        for fc in range(FC):
                            mm(bank_buf[bk], bank(bk, 0, 256),
                               [SB(slot), SB("xsb%d" % (fc // 4))],
                               B(slot, fc * 128, 128),
                               B("xsb", fc * SC + c0 * 128, 256),
                               fc == 0, fc == FC - 1)
                        vop((lambda oc, bk: lambda sc: sc.activation(
                            B("silug", oc * 256, 256), bank(bk, 0, 256), AF.Silu,
                            bias=F("bgc", oc, 1)))(oc, bk),
                            [bank_buf[bk], SB("bgc")], [SB("silug")], engine="scalar")
            def emit_v(hc):
                c0 = 2 * hc
                for vb in range(FOD):
                    for q in range(FC // 4):
                        p = (vb * (FC // 4) + q) % 4
                        slot = wp_slot[p]
                        gload("wp%d" % p, B(slot), d_wv[vb, q], SB(slot))
                        for sl in range(2):
                            c = c0 + sl
                            bk = 1 + ((vb % 2) * 2 + sl)
                            for f4 in range(4):
                                fc = 4 * q + f4
                                mm(bank_buf[bk], bank(bk),
                                   [SB(slot), SB("xsb%d" % (fc // 4))],
                                   B("xsb", fc * SC + c * 128, 128),
                                   B(slot, f4 * 512, 512),
                                   q == 0 and f4 == 0, q == FC // 4 - 1 and f4 == 3,
                                   force_tok=(sl == 1 and f4 == 3))
                    for sl in range(2):
                        bk = 1 + ((vb % 2) * 2 + sl)
                        vop((lambda sl, vb, bk: lambda v: v.tensor_copy(
                            B("vtm%d" % sl, vb * 512, 512), bank(bk)))(sl, vb, bk),
                            [bank_buf[bk]], [SB("vtm%d" % sl)])


            emit_g(0)
            emit_v(0)
            for sl in range(2):
                emit_gla_chunk(sci, 0, sl, main)
            emit_v(1)
            if main:
                emit_oproj_ln(sci, 0)
            emit_g(1)
            for sl in range(2):
                emit_gla_chunk(sci, 1, sl, main)
            if main:
                emit_oproj_ln(sci, 1)

        def emit_gla_chunk(sci, hc, sl, main):
            c = 2 * hc + sl
            vtm = "vtm%d" % sl
            for h in range(H):
                mm(bank_buf[0], pbbf[:, h * 128:(h + 1) * 128],
                   [SB("kdfm"), SB("identb")],
                   B("kdfm", h * SC + c * 128, 128),
                   None, h == 0, h == H - 1, transpose_in=B("identb"))
            vop(lambda v: v.tensor_copy(B("kdtm", 0, H * 128), pbbf[:, 0:H * 128]),
                [bank_buf[0]], [SB("kdtm")])

            if main:
                vop(lambda v: v.tensor_copy(B("Sbf"), F("S")),
                    [SB("S")], [SB("Sbf")])
                for h in range(H):
                    bk = 5 + h // 4
                    q4 = h % 4
                    nh = min(4, H - (h // 4) * 4)
                    mm(bank_buf[bk], bank(bk, q4 * 128, 128),
                       [SB("kneg"), SB("qd")],
                       B("kneg", h * SC + c * 128, 128),
                       B("qd", h * SC + c * 128, 128),
                       q4 == 0, q4 == nh - 1)
                for ab in range(NAB):
                    nh = min(4, H - ab * 4)
                    vop((lambda ab, nh: lambda v: v.tensor_tensor(
                        B("Asb", ab * 512, nh * 128), bank(5 + ab, 0, nh * 128),
                        F("masku", 0, nh * 128), op=ALU.mult))(ab, nh),
                        [bank_buf[5 + ab], SB("masku")], [SB("Asb")])
                cnt = {}
                total = {}
                for h in range(H):
                    for vdc in range(2):
                        bk = 1 + (vdc * HP + h * 128) // 512
                        total[bk] = total.get(bk, 0) + 2
                for h in range(H):
                    for vdc in range(2):
                        bk = 1 + (vdc * HP + h * 128) // 512
                        lo = (vdc * HP + h * 128) % 512
                        n = cnt.get(bk, 0)
                        mm(bank_buf[bk], bank(bk, lo, 128),
                           [SB(vtm), SB("Asb")],
                           B(vtm, h * 256 + vdc * 128, 128),
                           B("Asb", h * 128, 128),
                           n == 0, n + 1 == total[bk])
                        cnt[bk] = n + 1
                        n = cnt[bk]
                        mm(bank_buf[bk], bank(bk, lo, 128),
                           [SB("Sbf"), SB("qd")],
                           B("Sbf", h * 256 + vdc * 128, 128),
                           B("qd", h * SC + c * 128, 128),
                           n == 0, n + 1 == total[bk])
                        cnt[bk] = n + 1
            for h in range(H):
                bk = 7 if h % 2 == 0 else 6
                mm(bank_buf[bk], bank(bk, 0, 256),
                   [SB("kdtm"), SB(vtm)],
                   B("kdtm", h * 128, 128),
                   B(vtm, h * 256, 256),
                   True, True)
                vop((lambda h, c, bk: lambda v: v.scalar_tensor_tensor(
                    F("S", h * 256, 256), F("S", h * 256, 256),
                    F("dl", h * CPS + c, 1), bank(bk, 0, 256),
                    op0=ALU.mult, op1=ALU.add))(h, c, bk),
                    [SB("dl"), bank_buf[bk]], [SB("S")])

            if not main:
                return

            vop(lambda v: v.tensor_copy(B("obf"), pball[:, 0:2 * HP]),
                [bank_buf[b] for b in range(1, 1 + NBO)], [SB("obf")])
            vop(lambda sc: sc.activation(B("sqbf"), B("obf"), AF.Square),
                [SB("obf")], [SB("sqbf")], engine="scalar")
            NBK2 = (HP + 511) // 512
            rowmm = []
            kindless_off = {}
            for kind in range(2):
                src_n = "obf" if kind == 0 else "sqbf"
                for part in range(NP):
                    w = min(ROWW, HP - part * ROWW)
                    bk = 1 + kind * NBK2 + (part * ROWW) // 512
                    lo = (part * ROWW) % 512
                    kindless_off[(bk, lo)] = part * ROWW
                    for vdc in range(2):
                        rowmm.append((bk, lo, w, src_n, vdc))
            per_bank = {}
            for bk, *_ in rowmm:
                per_bank[bk] = per_bank.get(bk, 0) + 1
            seen = {}
            for bk, lo, w, src_n, vdc in rowmm:
                n = seen.get(bk, 0)
                part_off = kindless_off[(bk, lo)]
                mm(bank_buf[bk], bank(bk, lo, w, p0=0, pn=1),
                   [SB("onesbc"), SB(src_n)],
                   onesbc[:],
                   B(src_n, vdc * HP + part_off, w),
                   n == 0, n + 1 == per_bank[bk])
                seen[bk] = n + 1
            vop(lambda v: v.tensor_copy(B("rowbf", 0, 2 * HP, p0=0, pn=1),
                                        pball[0:1, 0:2 * HP]),
                [bank_buf[b] for b in range(1, 1 + 2 * NBK2)], [SB("rowbf")])
            for kind in range(2):
                for part in range(NP):
                    w = min(ROWW, HP - part * ROWW)
                    bk = 1 + kind * NBK2 + (part * ROWW) // 512
                    lo = (part * ROWW) % 512
                    first = (part * ROWW) % 512 == 0
                    last = (part == NP - 1) or ((part + 1) * ROWW) % 512 == 0
                    mm(bank_buf[bk], bank(bk, lo, w),
                       [SB("onesbr"), SB("rowbf")],
                       onesbr[:],
                       B("rowbf", kind * HP + part * ROWW, w, p0=0, pn=1),
                       first, last)
            MEAN, RSTD, RSG = 0, HP, 2 * HP
            sumbanks = [bank_buf[b] for b in range(1, 1 + NBK2)]
            sqbanks = [bank_buf[b] for b in range(1 + NBK2, 1 + 2 * NBK2)]
            vop(lambda v: v.tensor_scalar(F("gkc", MEAN, HP), pball[:, 0:HP],
                                          1.0 / 256, None, op0=ALU.mult),
                sumbanks, [SB("gkc")])
            vop(lambda v: v.tensor_tensor(F("gkc", RSG, HP), F("gkc", MEAN, HP),
                                          F("gkc", MEAN, HP), op=ALU.mult),
                [SB("gkc")], [SB("gkc")])
            vop((lambda: lambda v: v.scalar_tensor_tensor(
                F("gkc", RSTD, HP), pball[:, NBK2 * 512:NBK2 * 512 + HP], 1.0 / 256,
                F("gkc", RSG, HP), op0=ALU.mult, op1=ALU.subtract))(),
                sqbanks + [SB("gkc")], [SB("gkc")])
            vop(lambda sc: sc.activation(F("gkc", RSG, HP), F("gkc", RSTD, HP), AF.Ln, bias=F("eps", 0, 1)),
                [SB("gkc"), SB("eps")], [SB("gkc")], engine="scalar")
            vop(lambda sc: sc.activation(F("gkc", RSTD, HP), F("gkc", RSG, HP), AF.Exp, scale=-0.5),
                [SB("gkc")], [SB("gkc")], engine="scalar")
            og = "og%d" % sl
            for vdc in range(2):
                sil_ap = Bap("silug", vdc * 256 + sl * 128, [[512, H], [1, 128]])
                scr = "scr1" if vdc == 0 else "scr2"
                vop((lambda vdc, scr: lambda v: v.tensor_tensor(
                    F(scr, 0, HP), B("obf", vdc * HP, HP), F("gkc", MEAN, HP),
                    op=ALU.subtract))(vdc, scr),
                    [SB("obf"), SB("gkc")], [SB(scr)], engine="gpsimd")
                vop((lambda sil_ap: lambda v: v.tensor_tensor(
                    F("gkc", RSG, HP), F("gkc", RSTD, HP), sil_ap, op=ALU.mult))(sil_ap),
                    [SB("silug")], [SB("gkc")], engine="gpsimd")
                og_ap = Bap(og, vdc * 128, [[256, H], [1, 128]])
                vop((lambda og_ap, scr: lambda v: v.tensor_tensor(
                    og_ap, F(scr, 0, HP), F("gkc", RSG, HP), op=ALU.mult))(og_ap, scr),
                    [SB(scr), SB("gkc")], [SB(og)], engine="gpsimd")

        def emit_oproj_ln(sci, hc):
            for sl in range(2):
                ch = sci * CPS + 2 * hc + sl
                sload(F("xtmc%d" % sl), d_xtm[ch], SB("xtmc%d" % sl), stream="xtmc%d" % sl)
            for fob in range(FOD):
                for q in range(FC // 4):
                    p = (fob * (FC // 4) + q) % 4
                    slot = wp_slot[p]
                    gload("wp%d" % p, B(slot), d_wo[fob, q], SB(slot))
                    for sl in range(2):
                        og = "og%d" % sl
                        bk = 1 + (fob * 2 + sl) % 6
                        for f4 in range(4):
                            oc = 4 * q + f4
                            mm(bank_buf[bk], bank(bk),
                               [SB(slot), SB(og)],
                               B(og, oc * 128, 128),
                               B(slot, f4 * 512, 512),
                               q == 0 and f4 == 0, q == FC // 4 - 1 and f4 == 3,
                               force_tok=(sl == 1 and f4 == 3))
                for sl in range(2):
                    bk = 1 + (fob * 2 + sl) % 6
                    vop((lambda sl, fob, bk: lambda v: v.tensor_tensor(
                        F("rc%d" % sl, fob * 512, 512), F("xtmc%d" % sl, fob * 512, 512),
                        bank(bk), op=ALU.add))(sl, fob, bk),
                        [SB("xtmc%d" % sl), bank_buf[bk]], [SB("rc%d" % sl)])
            for sl in range(2):
                ch = sci * CPS + 2 * hc + sl
                emit_rowln(SB("rc%d" % sl), (lambda sl: lambda lo, w: F("rc%d" % sl, lo, w))(sl),
                           SB("xtmc%d" % sl), (lambda sl: lambda lo, w: F("xtmc%d" % sl, lo, w))(sl),
                           D, SB("x2c"), lambda lo, w: F("x2c", lo, w), "lnv")
                vop(lambda v: v.tensor_copy(B("x2bfc", 0, D), F("x2c", 0, D)),
                    [SB("x2c")], [SB("x2bfc")])
                store(d_x2tm[ch * 128:(ch + 1) * 128, :], SB("x2c"), F("x2c", 0, D), stream="x2tm")
                store(d_x2bf[ch * 128:(ch + 1) * 128, :], SB("x2bfc"), B("x2bfc", 0, D), stream="x2bf")

        def emit_rowln(rbuf, rap, dumpbuf, dumpap, W, obuf, oap, lnv, epsn="eps"):
            vop((lambda: lambda v: v.reduce_sum(F(lnv, 0, 1), rap(0, W), axis=AX.X))(),
                [rbuf], [SB(lnv)])
            vop((lambda: lambda sc: sc.activation(dumpap(0, W), rap(0, W), AF.Square,
                                                  accum_out=F(lnv, 2, 1)))(),
                [rbuf], [dumpbuf, SB(lnv)], engine="scalar")
            vop((lambda: lambda v: v.tensor_scalar(F(lnv, 1, 1), F(lnv, 0, 1),
                                                   1.0 / W, None, op0=ALU.mult))(),
                [SB(lnv)], [SB(lnv)])
            vop((lambda: lambda v: v.tensor_tensor(F(lnv, 3, 1), F(lnv, 1, 1),
                                                   F(lnv, 1, 1), op=ALU.mult))(),
                [SB(lnv)], [SB(lnv)])
            vop((lambda: lambda v: v.scalar_tensor_tensor(
                F(lnv, 4, 1), F(lnv, 2, 1), 1.0 / W, F(lnv, 3, 1),
                op0=ALU.mult, op1=ALU.subtract))(),
                [SB(lnv)], [SB(lnv)])
            vop((lambda epsn: lambda sc: sc.activation(F(lnv, 5, 1), F(lnv, 4, 1), AF.Ln,
                                                        bias=F(epsn, 0, 1)))(epsn),
                [SB(lnv), SB(epsn)], [SB(lnv)], engine="scalar")
            vop((lambda: lambda sc: sc.activation(F(lnv, 6, 1), F(lnv, 5, 1), AF.Exp, scale=-0.5))(),
                [SB(lnv)], [SB(lnv)], engine="scalar")
            vop((lambda: lambda v: v.tensor_scalar(
                oap(0, W), rap(0, W), F(lnv, 1, 1), F(lnv, 6, 1),
                op0=ALU.subtract, op1=ALU.mult))(),
                [rbuf, SB(lnv)], [obuf])

        for sci in range(NSCP):
            emit_attn_superchunk(sci, main=False)
        for sci in range(NSC):
            emit_attn_superchunk(sci, main=True)

        attn_end = P.boundary()
        PHASE["bf"] = bf_ffn
        PHASE["f32"] = f32_ffn
        for name in ("x2sb", "tsb", "wgb0", "wgb1", "wub0", "wub1", "wdb0", "wdb1",
                     "rwtsb_b", "routed", "siluscr0", "siluscr1", "smx0", "smx1", "comb",
                     "r2", "x2tmc", "out2", "lnv2", "eps2", "ygat0", "ygat1", "eqsv",
                     "wsave", "rktm", "iota16", "sprs", "identf", "xtmrow", "keepbf", "idx",
                     "identb2", "ygat0", "ygat1", "ygs0", "ygs1", "ygs2", "ygs3", "onesf2"):
            SB(name).readers = list(attn_end)

        gload("rwt", B("rwtsb", 0, FC * EW), d_rwt, SB("rwtsb_b"))
        vop(lambda v: v.memset(F("eps2", 0, 1), 1e-5), [], [SB("eps2")])
        sload(F("iota16", 0, 16), d_iota16, SB("iota16"), stream="iota")
        sload(F("identf"), d_identf, SB("identf"), stream="identf")
        gload("ident2", B("identb"), d_ident, SB("identb2"))
        vop(lambda v: v.memset(F("onesf2"), 1.0), [], [SB("onesf2")])

        bc_reg = [None]

        def bcreg(gp):
            if bc_reg[0] is None:
                bc_reg[0] = gp.to_reg(E * CC - 1)
            return bc_reg[0]

        def multi_writes(buf, emits):
            base = buf.wr_waits()
            toks = [fn(base) for fn in emits]
            buf.writers = [t for t in toks if t is not None]
            buf.readers = []
            return toks

        def load_x2sb(g):
            g0 = g * G
            def mk(fc):
                def fn(base):
                    return pop("sync", (lambda fc: lambda e: e.dma_start_transpose(
                        out=B("x2sb", fc * G, G),
                        in_=d_x2bf[g0:g0 + G, fc * 128:(fc + 1) * 128]))(fc),
                        waits=list(attn_end) + base, inc=gs("sl_x2sb"), inc_n=16)
                return fn
            multi_writes(SB("x2sb"), [mk(fc) for fc in range(FC)])

        def emit_router_sub(g, sub, bk):
            sp = sub % 2
            so = sp * (6 * 16 + 64)
            SM = lambda i: F("smx", so + i * 16, E)
            SV = lambda i: F("smx", so + 6 * 16 + i, 1)
            SMXB = "smx%d" % sp
            bkb = bank_buf[bk]
            vop((lambda SV, bk: lambda sc: sc.activation(
                SV(7), bank(bk, E, 1), AF.Exp, scale=-1.0))(SV, bk),
                [bkb], [SB(SMXB)], engine="scalar")
            vop((lambda SV: lambda v: v.tensor_scalar(SV(7), SV(7), 1.0, None, op0=ALU.add))(SV),
                [SB(SMXB)], [SB(SMXB)])
            vop((lambda g, sub, SV: lambda v: v.reciprocal(
                F("comb", (g * SG + sub) * EW + E, 1), SV(7)))(g, sub, SV),
                [SB(SMXB)], [SB("comb")])
            vop((lambda bk, SM: lambda v: v.tensor_copy(SM(0), bank(bk, 0, E)))(bk, SM),
                [bkb], [SB(SMXB)])
            vop((lambda SM, SV: lambda v: v.reduce_max(SV(0), SM(0), axis=AX.X, negate=True))(SM, SV),
                [SB(SMXB)], [SB(SMXB)])
            vop((lambda SM, SV: lambda sc: sc.activation(SM(1), SM(0), AF.Exp, bias=SV(0)))(SM, SV),
                [SB(SMXB)], [SB(SMXB)], engine="scalar")
            vop((lambda SM, SV: lambda v: v.reduce_sum(SV(1), SM(1), axis=AX.X))(SM, SV),
                [SB(SMXB)], [SB(SMXB)])
            vop((lambda SV: lambda v: v.reciprocal(SV(2), SV(1)))(SV),
                [SB(SMXB)], [SB(SMXB)])
            vop((lambda SM, SV: lambda v: v.tensor_scalar(SM(0), SM(1), SV(2), None, op0=ALU.mult))(SM, SV),
                [SB(SMXB)], [SB(SMXB)])
            vop((lambda SM: lambda v: v.tensor_copy(SM(2), SM(0)))(SM), [SB(SMXB)], [SB(SMXB)])
            for it in range(4):
                wsrc, wdst = (2, 3) if it % 2 == 0 else (3, 2)
                vop((lambda wsrc, SM, SV: lambda v: v.reduce_max(SV(3), SM(wsrc), axis=AX.X))(wsrc, SM, SV),
                    [SB(SMXB)], [SB(SMXB)])
                vop((lambda sub, it, SV: lambda v: v.tensor_copy(
                    F("wsave", (g * SG + sub) * 4 + it, 1), SV(3)))(sub, it, SV),
                    [SB(SMXB)], [SB("wsave")])
                vop((lambda wsrc, sub, it, SM, SV: lambda v: v.tensor_scalar(
                    F("eqsv", (sub * 4 + it) * 16, 16, w=E) if False else
                    F("eqsv", (sub * 4 + it) * 16, E),
                    SM(wsrc), SV(3), None, op0=ALU.is_ge))(wsrc, sub, it, SM, SV),
                    [SB(SMXB)], [SB("eqsv")])
                if it < 3:
                    vop((lambda wsrc, wdst, sub, it, SM: lambda v: v.scalar_tensor_tensor(
                        SM(wdst), F("eqsv", (sub * 4 + it) * 16, E), -2.0, SM(wsrc),
                        op0=ALU.mult, op1=ALU.add))(wsrc, wdst, sub, it, SM),
                        [SB(SMXB), SB("eqsv")], [SB(SMXB)])
            vop((lambda SM, SV: lambda v: v.tensor_scalar(SM(4), SM(0), SV(3), None, op0=ALU.is_ge))(SM, SV),
                [SB(SMXB)], [SB(SMXB)])
            vop((lambda SM: lambda v: v.tensor_tensor(SM(1), SM(0), SM(4), op=ALU.mult))(SM),
                [SB(SMXB)], [SB(SMXB)])
            vop((lambda SM, SV: lambda v: v.reduce_sum(SV(4), SM(1), axis=AX.X))(SM, SV),
                [SB(SMXB)], [SB(SMXB)])
            vop((lambda SV: lambda v: v.tensor_scalar(SV(5), SV(4), 1e-9, None, op0=ALU.max))(SV),
                [SB(SMXB)], [SB(SMXB)])
            vop((lambda SV: lambda v: v.reciprocal(SV(6), SV(5)))(SV),
                [SB(SMXB)], [SB(SMXB)])
            for k in range(4):
                vop((lambda sub, k, SV: lambda v: v.tensor_scalar(
                    F("wsave", (g * SG + sub) * 4 + k, 1),
                    F("wsave", (g * SG + sub) * 4 + k, 1), SV(6), None, op0=ALU.mult))(sub, k, SV),
                    [SB(SMXB)], [SB("wsave")])
            vop((lambda SM: lambda v: v.tensor_copy(B("keepbf", 0, E), SM(4)))(SM),
                [SB(SMXB)], [SB("keepbf")])
            mm(bank_buf[0], pbbf[0:E, 0:128],
               [SB("keepbf"), SB("identb2")],
               B("keepbf", 0, E), None, True, True, transpose_in=B("identb"))
            vop((lambda sub: lambda v: v.tensor_copy(
                F("r2", sub * 128, 128, p0=0, pn=E), pbbf[0:E, 0:128]))(sub),
                [bank_buf[0]], [SB("r2")])

        def MW(col, w, pn=16):
            o, _ = f32_ffn["r2"]
            return af32[0:pn, o + col:o + col + w]

        def emit_sparse_prep(g):
            for i in range(SG):
                init = 0.0 if i == 0 else MW(G + i * 128 - 1, 1)
                vop((lambda i, init: lambda v: v.tensor_tensor_scan(
                    MW(G + i * 128, 128),
                    F("onesf2", 0, 128, p0=0, pn=16),
                    MW(i * 128, 128),
                    init, op0=ALU.mult, op1=ALU.add))(i, init),
                    [SB("onesf2"), SB("r2"), SB("x2tmc")], [SB("r2")])
            vop(lambda v: v.tensor_tensor(
                MW(2 * G, G), MW(G, G), MW(0, G), op=ALU.subtract),
                [SB("r2")], [SB("r2"), SB("x2tmc")])
            for sub in range(SG):
                mm(bank_buf[7], bank(7, 0, 16),
                   [SB("r2"), SB("x2tmc"), SB("identf")],
                   MW(2 * G + sub * 128, 128),
                   None, True, True, transpose_in=F("identf", 0, 16, p0=0, pn=16))
                vop(lambda v: v.tensor_copy(F("rktm", 0, 16), bank(7, 0, 16)),
                    [bank_buf[7]], [SB("rktm")])
                sload(B("xtmrow", 0, D), d_x2bf[g * G + sub * 128:g * G + (sub + 1) * 128, :],
                      SB("xtmrow"), stream="xtr")
                for k in range(4):
                    eq = (lambda sub, k: F("eqsv", (sub * 4 + k) * 16, E))(sub, k)
                    vop((lambda sub, k: lambda v: v.tensor_tensor(
                        F("sprs", 0, 16, w=E) if False else F("sprs", 0, E),
                        F("eqsv", (sub * 4 + k) * 16, E), F("iota16", 0, E), op=ALU.mult))(sub, k),
                        [SB("eqsv"), SB("iota16")], [SB("sprs")])
                    vop(lambda v: v.reduce_sum(F("sprs", 16, 1), F("sprs", 0, E), axis=AX.X),
                        [SB("sprs")], [SB("sprs")])
                    vop((lambda sub, k: lambda v: v.tensor_tensor(
                        F("sprs", 0, E), F("eqsv", (sub * 4 + k) * 16, E), F("rktm", 0, E), op=ALU.mult))(sub, k),
                        [SB("eqsv"), SB("rktm")], [SB("sprs")])
                    vop(lambda v: v.reduce_sum(F("sprs", 17, 1), F("sprs", 0, E), axis=AX.X),
                        [SB("sprs")], [SB("sprs")])
                    vop(lambda v: v.scalar_tensor_tensor(
                        F("sprs", 18, 1), F("sprs", 16, 1), float(CC), F("sprs", 17, 1),
                        op0=ALU.mult, op1=ALU.add),
                        [SB("sprs")], [SB("sprs")])
                    vop((lambda g: lambda v: v.tensor_scalar(
                        F("sprs", 19, 1), F("sprs", 18, 1), float(g * C + 1e7), None, op0=ALU.add))(g),
                        [SB("sprs")], [SB("sprs")])
                    vop(lambda v: v.tensor_scalar(
                        F("sprs", 20, 1), F("sprs", 17, 1), float(C) - 0.5, None, op0=ALU.is_le),
                        [SB("sprs")], [SB("sprs")])
                    vop(lambda v: v.scalar_tensor_tensor(
                        F("sprs", 21, 1), F("sprs", 20, 1), -1e7, F("sprs", 19, 1),
                        op0=ALU.mult, op1=ALU.add),
                        [SB("sprs")], [SB("sprs")])
                    col = (g * SG + sub) * 4 + k
                    vop((lambda col: lambda v: v.tensor_copy(idxsb[:, col:col + 1], F("sprs", 21, 1)))(col),
                        [SB("sprs")], [SB("idx")])
                    tok = pop("gpsimd", (lambda col: lambda gp: gp.indirect_dma_start(
                        out=d_xc, out_offset=bass.IndirectOffsetOnAxis(idxsb[:, col:col + 1], 0),
                        in_=B("xtmrow", 0, D), in_offset=None,
                        bounds_check=bcreg(gp), oob_is_err=False))(col),
                        waits=SB("idx").rd_waits() + SB("xtmrow").rd_waits() +
                        [SB("sct%d" % k).writers[-1] if SB("sct%d" % k).writers else None,
                         Tok(gs("st_zxc"), gs("st_zxc").count)],
                        inc=gs("sct%d" % k), inc_n=16)
                    SB("idx").add_reader(tok)
                    SB("xtmrow").add_reader(tok)
                    SB("sct%d" % k).set_writer(tok)

        def emit_xc_transposes(e):
            def mk(fc):
                def fn(base):
                    return pop("sync", (lambda fc: lambda ee: ee.dma_start_transpose(
                        out=B("x2sb", fc * CC, CC),
                        in_=d_xc[e * CC:(e + 1) * CC, fc * 128:(fc + 1) * 128]))(fc),
                        waits=base + scat_end, inc=gs("sl_xct"), inc_n=16)
                return fn
            multi_writes(SB("x2sb"), [mk(fc) for fc in range(FC)])

        def emit_sparse_expert2(e):
            for ib in range(IB):
                p = ib % 2
                gload("wgb%d" % p, B("wgb%d" % p), d_wgate[e, ib], SB("wgb%d" % p))
                gload("wub%d" % p, B("wub%d" % p), d_wup[e, ib], SB("wub%d" % p))
                gul = {}
                for kind in range(2):
                    wname = ("wgb%d" if kind == 0 else "wub%d") % p
                    for h, (h0, hw) in enumerate(CHALF):
                        bk = 1 + (p * 2 * NCHALF + kind * NCHALF + h) % 7
                        gul[(kind, h)] = bk
                        for fc in range(FC):
                            mm(bank_buf[bk], bank(bk, 0, hw),
                               [SB(wname), SB("x2sb")],
                               B(wname, fc * 128, 128),
                               B("x2sb", fc * CC + h0, hw),
                               fc == 0, fc == FC - 1)
                        if kind == 0:
                            vop((lambda h, hw, bk: lambda sc: sc.activation(
                                F("siluscr", h * 512, hw), bank(bk, 0, hw), AF.Silu))(h, hw, bk),
                                [bank_buf[bk]], [SB("siluscr%d" % h)], engine="scalar")
                for h, (h0, hw) in enumerate(CHALF):
                    ubk = gul[(1, h)]
                    vop((lambda ib, h0, hw, ubk, h: lambda v: v.tensor_tensor(
                        B("tsb", ib * CC + h0, hw), F("siluscr", h * 512, hw),
                        bank(ubk, 0, hw), op=ALU.mult))(ib, h0, hw, ubk, h),
                        [SB("siluscr%d" % h), bank_buf[ubk]], [SB("tsb")])
            if e + 1 < E:
                emit_xc_transposes(e + 1)
            for fo in range(FOD):
                p = fo % 2
                gload("wdb%d" % p, B("wdb%d" % p), d_wdown[e, fo], SB("wdb%d" % p))
                for sub in range(CSUB):
                    bk = 1 + (fo * CSUB + sub) % 7
                    for ib in range(IB):
                        mm(bank_buf[bk], bank(bk),
                           [SB("tsb"), SB("wdb%d" % p)],
                           B("tsb", ib * CC + sub * 128, 128),
                           B("wdb%d" % p, ib * 512, 512),
                           ib == 0, ib == IB - 1)
                    ysl = (sub % 2) * 2 + fo % 2
                    vop((lambda bk, ysl: lambda v: v.tensor_copy(
                        B("ygat", ysl * 512, 512), bank(bk)))(bk, ysl),
                        [bank_buf[bk]], [SB("ygs%d" % ysl)])
                    store(d_yc[e * CC + sub * 128:e * CC + (sub + 1) * 128,
                               fo * 512:(fo + 1) * 512],
                          SB("ygs%d" % ysl), B("ygat", ysl * 512, 512),
                          stream="yc%d" % ysl)

        def seed_ygat():
            for yp in range(2):
                for s4 in range(4):
                    SB("ygat%d" % yp).readers += SB("ygs%d" % s4).readers
                    SB("ygat%d" % yp).writers += SB("ygs%d" % s4).writers

        def init_routed():
            base = SB("routed")
            ws = base.wr_waits()
            toks = []
            for s in range(SG):
                toks.append(pop("vector", (lambda s: lambda v: v.memset(F("routed", s * D, D), 0.0))(s),
                                waits=ws, inc=dve))
            base.writers = toks
            base.readers = []

        def emit_shared(g):
            for ee in range(E, E + SHQ):
                emit_dense_expert(g, ee)

        def emit_backward_sub(g, sub):
            if True:
                for k in range(4):
                    yp = k % 2
                    col = (g * SG + sub) * 4 + k
                    gtok = pop("gpsimd", (lambda col, yp: lambda gp: gp.indirect_dma_start(
                        out=B("ygat", yp * D, D), out_offset=None,
                        in_=d_yc, in_offset=bass.IndirectOffsetOnAxis(idxsb[:, col:col + 1], 0),
                        bounds_check=bcreg(gp), oob_is_err=False))(col, yp),
                        waits=SB("ygat%d" % yp).wr_waits() + SB("idx").rd_waits() + yc_end,
                        inc=gs("gyc%d" % yp), inc_n=16)
                    SB("ygat%d" % yp).set_writer(gtok)
                    SB("idx").add_reader(gtok)
                    vop((lambda sub, k, yp, col: lambda v: v.scalar_tensor_tensor(
                        F("routed", sub * D, D), B("ygat", yp * D, D),
                        F("wsave", col, 1), F("routed", sub * D, D),
                        op0=ALU.mult, op1=ALU.add))(sub, k, yp, col),
                        [SB("ygat%d" % yp), SB("wsave"), SB("routed")], [SB("routed")])

        def emit_combine_sub(g, sub):
            if True:
                ch = g * SG + sub
                sload(F("x2tmc"), d_x2tm[ch * 128:(ch + 1) * 128, :], SB("x2tmc"),
                      extra=attn_end, stream="x2tmc")
                vop((lambda sub: lambda v: v.tensor_tensor(
                    F("r2", 0, D), F("x2tmc", 0, D), F("routed", sub * D, D), op=ALU.add))(sub),
                    [SB("x2tmc"), SB("routed")], [SB("r2")])
                emit_rowln(SB("r2"), lambda lo, w: F("r2", lo, w),
                           SB("x2tmc"), lambda lo, w: F("x2tmc", lo, w),
                           D, SB("out2"), lambda lo, w: F("out2", lo, w), "lnv2", epsn="eps2")
                store(d_out[ch * 128:(ch + 1) * 128, :], SB("out2"), F("out2", 0, D), stream="out")

        def emit_dense_expert(g, ee):
            qq = ee - E
            wg_src = lambda ib: d_shg[qq * IB + ib]
            wu_src = lambda ib: d_shu[qq * IB + ib]
            wd_src = lambda fo: d_shd[qq, fo]
            col = lambda sub: (g * SG + sub) * EW + E
            for ib in range(IB):
                p = ib % 2
                gload("wgb%d" % p, B("wgb%d" % p), wg_src(ib), SB("wgb%d" % p))
                gload("wub%d" % p, B("wub%d" % p), wu_src(ib), SB("wub%d" % p))
                gul = {}
                for kind in range(2):
                    wname = ("wgb%d" if kind == 0 else "wub%d") % p
                    for h, (h0, hw) in enumerate(GHALF):
                        bk = 1 + (p * 2 * NHALF + kind * NHALF + h) % 7
                        gul[(kind, h)] = bk
                        for fc in range(FC):
                            mm(bank_buf[bk], bank(bk, 0, hw),
                               [SB(wname), SB("x2sb")],
                               B(wname, fc * 128, 128),
                               B("x2sb", fc * G + h0, hw),
                               fc == 0, fc == FC - 1)
                        if kind == 0:
                            vop((lambda h, hw, bk: lambda sc: sc.activation(
                                F("siluscr", h * 512, hw), bank(bk, 0, hw), AF.Silu))(h, hw, bk),
                                [bank_buf[bk]], [SB("siluscr%d" % h)], engine="scalar")
                for h, (h0, hw) in enumerate(GHALF):
                    ubk = gul[(1, h)]
                    vop((lambda ib, h0, hw, ubk, h: lambda v: v.tensor_tensor(
                        B("tsb", ib * G + h0, hw), F("siluscr", h * 512, hw),
                        bank(ubk, 0, hw), op=ALU.mult))(ib, h0, hw, ubk, h),
                        [SB("siluscr%d" % h), bank_buf[ubk]], [SB("tsb")])
            for fo in range(FOD):
                p = fo % 2
                gload("wdb%d" % p, B("wdb%d" % p), wd_src(fo), SB("wdb%d" % p))
                for sub in range(SG):
                    bk = 1 + (fo * SG + sub) % 7
                    for ib in range(IB):
                        mm(bank_buf[bk], bank(bk),
                           [SB("tsb"), SB("wdb%d" % p)],
                           B("tsb", ib * G + sub * 128, 128),
                           B("wdb%d" % p, ib * 512, 512),
                           ib == 0, ib == IB - 1)
                    vop((lambda sub, fo, bk, cc: lambda v: v.scalar_tensor_tensor(
                        F("routed", sub * D + fo * 512, 512), bank(bk),
                        F("comb", cc, 1),
                        F("routed", sub * D + fo * 512, 512),
                        op0=ALU.mult, op1=ALU.add))(sub, fo, bk, col(sub)),
                        [bank_buf[bk], SB("comb"), SB("routed")], [SB("routed")])

        def emit_bc_interleaved(g):
            for sub in range(SG):
                emit_backward_sub(g, sub)
                emit_combine_sub(g, sub)

        def emit_router_group(g):
            load_x2sb(g)
            for sub in range(SG):
                bk = [5, 6][sub % 2]
                for fc in range(FC):
                    mm(bank_buf[bk], bank(bk, 0, EW),
                       [SB("rwtsb_b"), SB("x2sb")],
                       B("x2sb", fc * G + sub * 128, 128),
                       B("rwtsb", fc * EW, EW),
                       fc == 0, fc == FC - 1)
                emit_router_sub(g, sub, bk)

        emit_router_group(0)
        emit_sparse_prep(0)
        emit_router_group(1)
        emit_sparse_prep(1)
        init_routed()
        emit_shared(1)
        scat_end = [Tok(gs("sct%d" % k), gs("sct%d" % k).count) for k in range(4)] + \
            [Tok(gs("st_zxc"), gs("st_zxc").count)]
        emit_xc_transposes(0)
        for e in range(E):
            emit_sparse_expert2(e)
        yc_end = [Tok(gs("st_yc%d" % i), gs("st_yc%d" % i).count) for i in range(4)]
        seed_ygat()
        emit_bc_interleaved(1)
        load_x2sb(0)
        init_routed()
        emit_shared(0)
        emit_bc_interleaved(0)

        final_waits = [Tok(s, s.count) for n, s in gsem.items() if n.startswith("st_")]
        P.op("sync", lambda e: e.nop(), waits=final_waits)
        P.emit(nc, es)
    return nc



def stage_core(x_main, x_prefix, cfg):
    M, D, FC, SC, NSC, NSCP = cfg["M"], cfg["D"], cfg["FC"], cfg["SC"], cfg["NSC"], cfg["NSCP"]
    def fm(x, nsc):
        xt = np.ascontiguousarray(x.T)
        return np.ascontiguousarray(
            xt.reshape(FC, 128, nsc, SC).transpose(2, 1, 0, 3)).reshape(nsc, 128, FC * SC)
    return {
        "xfm": fm(x_main, NSC),
        "xpfm": fm(x_prefix, max(NSCP, 1)),
        "xtm": np.ascontiguousarray(x_main.reshape(cfg["NCH"], 128, D)),
    }


def stage_weights(w, cfg):
    D, FC, H, E, IB, SHQ, FOD = (cfg["D"], cfg["FC"], cfg["H"], cfg["E"], cfg["IB"],
                                 cfg["SHQ"], cfg["FOD"])
    st = {}
    def occhunks(W, nout):
        return np.ascontiguousarray(
            W.reshape(FC, 128, nout, 128).transpose(2, 1, 0, 3)).reshape(nout, 128, FC * 128)
    def quarters(W):
        return np.ascontiguousarray(
            W.reshape(FC // 4, 4, 128, FOD, 512).transpose(3, 0, 2, 1, 4)).reshape(
                FOD, FC // 4, 128, 4 * 512)
    st["wq"] = occhunks(w["Wq"], H)
    st["wk"] = occhunks(w["Wk"], H)
    st["wkg1"] = np.ascontiguousarray(
        w["Wkg1"].reshape(FC, 128, 16).transpose(1, 0, 2)).reshape(128, FC * 16)
    st["wkg2"] = np.ascontiguousarray(w["Wkg2"])
    st["bkg2s"] = np.ascontiguousarray(w["bkg2"].reshape(H, 128).T)
    st["wv"] = quarters(w["Wv"])
    st["wg"] = occhunks(w["Wg"], FC)
    st["bgs"] = np.ascontiguousarray(w["bg"].reshape(FC, 128).T)
    st["wo"] = quarters(w["Wo"])
    rw_sg = np.concatenate([w["router_w"].T, w["sh_gate_w"]], axis=1)
    ew = E + 1
    st["rwt"] = np.ascontiguousarray(
        rw_sg.reshape(FC, 128, ew).transpose(1, 0, 2)).reshape(128, FC * ew)
    st["wgate"] = np.ascontiguousarray(
        w["w_gate"].reshape(E, FC, 128, IB, 128).transpose(0, 3, 2, 1, 4)).reshape(
            E, IB, 128, FC * 128)
    st["wup"] = np.ascontiguousarray(
        w["w_up"].reshape(E, FC, 128, IB, 128).transpose(0, 3, 2, 1, 4)).reshape(
            E, IB, 128, FC * 128)
    st["wdown"] = np.ascontiguousarray(
        w["w_down"].reshape(E, IB, 128, FOD, 512).transpose(0, 3, 2, 1, 4)).reshape(
            E, FOD, 128, IB * 512)
    st["shg"] = np.ascontiguousarray(
        w["sh_gate"].reshape(FC, 128, SHQ * IB, 128).transpose(2, 1, 0, 3)).reshape(
            SHQ * IB, 128, FC * 128)
    st["shu"] = np.ascontiguousarray(
        w["sh_up"].reshape(FC, 128, SHQ * IB, 128).transpose(2, 1, 0, 3)).reshape(
            SHQ * IB, 128, FC * 128)
    st["shd"] = np.ascontiguousarray(
        w["sh_down"].reshape(SHQ, IB, 128, FOD, 512).transpose(0, 3, 2, 1, 4)).reshape(
            SHQ, FOD, 128, IB * 512)
    st["iota16"] = np.ascontiguousarray(np.tile(np.arange(16, dtype=np.float32), (128, 1)))
    st["identf"] = np.eye(128, dtype=np.float32)
    st["masku4"] = np.ascontiguousarray(
        np.tile(np.triu(np.ones((128, 128), np.float32)), (1, 4)))
    st["identbf"] = np.eye(128).astype(BFD)
    return st


def kernel(**inputs):
    cfg = make_cfg(mini=False)
    x = np.asarray(inputs["L0_attn_x"])
    w = {
        "Wq": inputs["L0_attn_Wq"], "Wk": inputs["L0_attn_Wk"],
        "Wkg1": inputs["L0_attn_Wkg1"], "Wkg2": inputs["L0_attn_Wkg2"],
        "bkg2": inputs["L0_attn_bkg2"], "Wv": inputs["L0_attn_Wv"],
        "Wg": inputs["L0_attn_Wg"], "bg": inputs["L0_attn_bg"],
        "Wo": inputs["L0_attn_Wo"], "router_w": inputs["L0_ffn_router_w"],
        "w_gate": inputs["L0_ffn_w_gate"], "w_up": inputs["L0_ffn_w_up"],
        "w_down": inputs["L0_ffn_w_down"], "sh_gate": inputs["L0_ffn_sh_gate"],
        "sh_up": inputs["L0_ffn_sh_up"], "sh_down": inputs["L0_ffn_sh_down"],
        "sh_gate_w": inputs["L0_ffn_sh_gate_w"],
    }
    w = {k: np.asarray(v, np.float32) for k, v in w.items()}
    wst = stage_weights(w, cfg)
    nc = build_program(cfg)
    in_maps = []
    zeros_pfx = np.zeros((cfg["PFX"], cfg["D"]), np.float32)
    for core in range(8):
        b, half = core // 2, core % 2
        xm = np.ascontiguousarray(x[b, half * 2048:(half + 1) * 2048, :], dtype=np.float32)
        xp = (np.ascontiguousarray(x[b, 0:2048, :], dtype=np.float32)
              if half == 1 else zeros_pfx)
        m = dict(wst)
        m.update(stage_core(xm, xp, cfg))
        in_maps.append(m)
    res = run_bass_kernel_spmd(nc, in_maps, core_ids=list(range(8)))
    out = np.empty((4, 4096, 2048), np.float32)
    for core in range(8):
        b, half = core // 2, core % 2
        out[b, half * 2048:(half + 1) * 2048, :] = res.results[core]["out"]
    return out
